# version 05
# baseline (speedup 1.0000x reference)
import numpy as np
from contextlib import ExitStack

import concourse.bass as bass
import concourse.mybir as mybir
from concourse.bass_utils import run_bass_kernel_spmd

F32 = mybir.dt.float32
BF16 = mybir.dt.bfloat16
U32 = mybir.dt.uint32
AF = mybir.ActivationFunctionType
ALU = mybir.AluOpType
AX = mybir.AxisListType
IOX = bass.IndirectOffsetOnAxis

NCORES = 8
H = 2048
HK = 16
TOK = 1024
HALO = 128
THAL = 1280
NTT = 10
OWN0 = 1
NOT_ = 8
NHEAD = 16
DH = 128
NEXP = 32
CAP = 192
NSLOT = NEXP * CAP
II = 1024
KW = 384
SCALE = float(DH) ** -0.5

DEBUG = []
PHASES = 99


class Ev:
    __slots__ = ("sem", "val", "needed", "inc")
    def __init__(self, sem, inc):
        self.sem = sem
        self.val = None
        self.needed = False
        self.inc = inc


class KB:
    ENGINES = ("tensor", "vector", "scalar", "gpsimd", "sync")

    def __init__(self, nc, stack):
        self.nc = nc
        self.stack = stack
        self.sems = {}
        self.queue = {e: [] for e in self.ENGINES}
        self.sem_ops = {}
        self.eng_sem = {}
        for e in self.ENGINES:
            n = "E_" + e
            self._mksem(n)
            self.eng_sem[e] = n
        self.res = {}
        self.named = {}

    def _mksem(self, name):
        self.sems[name] = self.stack.enter_context(self.nc.semaphore(name))
        self.sem_ops[name] = []

    def chan(self, name):
        n = "C_" + name
        if n not in self.sems:
            self._mksem(n)
        return n

    def _collect(self, deps, reads, writes):
        evs = list(deps)
        for r in reads:
            e = self.res.get(r)
            if e and e[0] is not None:
                evs.append(e[0])
        for w in writes:
            e = self.res.get(w)
            if e:
                if e[0] is not None:
                    evs.append(e[0])
                evs.extend(e[1])
        for e in evs:
            e.needed = True
        return evs

    def _post(self, ev, reads, writes):
        for w in writes:
            self.res[w] = [ev, []]
        for r in reads:
            self.res.setdefault(r, [None, []])[1].append(ev)

    def op(self, engine, fn, deps=(), reads=(), writes=()):
        evs = self._collect(deps, reads, writes)
        ev = Ev(self.eng_sem[engine], 1)
        self.sem_ops[ev.sem].append(ev)
        self.queue[engine].append((evs, fn, ev))
        self._post(ev, reads, writes)
        return ev

    def dma(self, engine, channel, fn, deps=(), reads=(), writes=()):
        evs = self._collect(deps, reads, writes)
        ev = Ev(self.chan(channel), 16)
        ev.needed = True
        self.sem_ops[ev.sem].append(ev)
        self.queue[engine].append((evs, fn, ev))
        self._post(ev, reads, writes)
        return ev

    def barrier_ev(self, *names):
        evs = []
        for n in names:
            e = self.res.get(n)
            if e:
                if e[0] is not None:
                    evs.append(e[0])
                evs.extend(e[1])
        return evs

    def all_current_events(self):
        return [lst[-1] for lst in self.sem_ops.values() if lst]

    def finalize_and_emit(self, block, final_engine="sync"):
        for sem, evs in self.sem_ops.items():
            cum = 0
            for ev in evs:
                if ev.needed:
                    cum += ev.inc
                    ev.val = cum
        finals = []
        for sem, evs in self.sem_ops.items():
            last = None
            for ev in evs:
                if ev.needed:
                    last = ev
            if last is not None:
                finals.append(last)
        kb = self

        def make_runner(engine):
            ops = kb.queue[engine]
            is_final = engine == final_engine

            def run(eng):
                hw = {}
                for (evs, fn, ev) in ops:
                    todo = {}
                    for e in evs:
                        if e.val is None:
                            continue
                        if hw.get(e.sem, 0) >= e.val:
                            continue
                        if todo.get(e.sem, 0) < e.val:
                            todo[e.sem] = e.val
                    for s, v in sorted(todo.items()):
                        eng.wait_ge(kb.sems[s], v)
                        hw[s] = v
                    inst = fn(eng)
                    if ev.val is not None:
                        inst.then_inc(kb.sems[ev.sem], ev.inc)
                if is_final:
                    for e in finals:
                        if e.sem != kb.eng_sem[engine]:
                            eng.wait_ge(kb.sems[e.sem], e.val)
            return run

        for engine in self.ENGINES:
            getattr(block, engine)(make_runner(engine))


def build_program():
    nc = bass.Bass("TRN2", target_bir_lowering=False)

    d_x = nc.dram_tensor("x_halo", [THAL, H], F32, kind="ExternalInput").ap()
    d_mask = nc.dram_tensor("masks", [NOT_ * 128, KW], F32, kind="ExternalInput").ap()
    d_wq = nc.dram_tensor("Wq", [NHEAD, 128, HK, DH], F32, kind="ExternalInput").ap()
    d_wk = nc.dram_tensor("Wk", [NHEAD, 128, HK, DH], F32, kind="ExternalInput").ap()
    d_wv = nc.dram_tensor("Wv", [NHEAD, 128, HK, DH], F32, kind="ExternalInput").ap()
    d_wo = nc.dram_tensor("Wo", [4, 128, HK, 512], F32, kind="ExternalInput").ap()
    d_nw = nc.dram_tensor("norm_w", [1, H], F32, kind="ExternalInput").ap()
    d_gwt = nc.dram_tensor("gate_wT", [H, NEXP], F32, kind="ExternalInput").ap()
    d_w13 = nc.dram_tensor("w13", [NEXP, 4, 128, HK, 512], F32, kind="ExternalInput").ap()
    d_w2 = nc.dram_tensor("w2", [NEXP, 4, 128, 8, 512], F32, kind="ExternalInput").ap()
    d_sg = nc.dram_tensor("sgw", [2, 128, HK, 512], F32, kind="ExternalInput").ap()
    d_su = nc.dram_tensor("suw", [2, 128, HK, 512], F32, kind="ExternalInput").ap()
    d_sd = nc.dram_tensor("sdw", [4, 128, 8, 512], F32, kind="ExternalInput").ap()
    d_out = nc.dram_tensor("out", [TOK, H], F32, kind="ExternalOutput").ap()
    d_X = nc.dram_tensor("Xbuf", [NSLOT, H], BF16, kind="Internal").ap()
    d_Y = nc.dram_tensor("Ybuf", [NSLOT, H], BF16, kind="Internal").ap()


    with ExitStack() as stack:
        kb = KB(nc, stack)
        sb = lambda name, shape, dt: stack.enter_context(nc.sbuf_tensor(name, shape, dt))
        psm = lambda name, shape, dt: stack.enter_context(nc.psum_tensor(name, shape, dt))

        acc = sb("acc", [128, NOT_ * H], F32)
        ARQ = sb("ARQ", [128, 20480], BF16)
        ARH = sb("ARH", [128, 16384], BF16)
        AR1 = sb("AR1", [128, 16384], BF16)
        WK = sb("WK", [128, 12288], BF16)
        MS = sb("MS", [128, 7808], BF16)

        def V(arena, off, n, dt=BF16, p0=0, p1=128):
            a = arena[p0:p1, off:off + n]
            return a.bitcast(dt) if dt != BF16 else a

        w_bc = V(MS, 0, 4096, F32)
        masks_bf = V(MS, 0, 3072)
        nw_row = V(MS, 0, 4096, F32, 0, 1)
        ident = V(MS, 4096, 256, F32)
        ident_bf = V(MS, 4352, 128)
        scal = V(MS, 4480, 128, F32)
        econst = V(MS, 4608, 8, F32)
        iota32 = V(MS, 4616, 64, F32)
        topv = V(MS, 4680, 64, F32)
        wrt = V(MS, 4744, 64, F32)
        posc = V(MS, 4808, 4, F32)
        addrf = V(MS, 4812, 2, F32)
        ovf = V(MS, 4816, 2, F32)
        addrU = V(MS, 4820, 64, U32)
        efc = V(MS, 4884, 64, F32)
        maxv8 = V(MS, 4948, 16, F32)
        maxi8 = V(MS, 4964, 16, U32)
        logit_t = V(MS, 4980, 64, F32)
        A_t = V(MS, 5044, 64, F32)
        oh_t = V(MS, 5108, 64, F32)
        pre_t = V(MS, 5172, 64, F32)
        gw_hi = V(MS, 5236, 512)
        gw_lo = V(MS, 5748, 512)
        smx0 = V(MS, 6260, 768, F32)
        ones1 = V(MS, 6260, 256, F32, 0, 1)
        den4 = V(MS, 7028, 8, F32)
        P_bf = V(MS, 7036, 384)
        PT_bf = V(MS, 7420, 384)

        ps = [psm(f"ps{i}", [128, 512], F32) for i in range(5)]
        psT0 = psm("psT0", [128, 128], BF16)
        psT1 = psm("psT1", [128, 128], BF16)
        psTf = psm("psTf", [128, 128], F32)

        kb.named.update(acc=acc, ARQ=ARQ, ARH=ARH, AR1=AR1, WK=WK, MS=MS, Xbuf=d_X, Ybuf=d_Y)

        C = lambda i, j: scal[:, i * 6 + j: i * 6 + j + 1]

        kb.op("gpsimd", lambda g: g.iota(ident[:], pattern=[[1, 128]], base=0,
                                         channel_multiplier=-1,
                                         allow_small_or_imprecise_dtypes=True),
              writes=("ident",))
        kb.op("gpsimd", lambda g: g.tensor_single_scalar(ident[:], ident[:], 0.0, ALU.is_equal),
              reads=("ident",), writes=("ident",))
        kb.op("vector", lambda v: v.tensor_copy(ident_bf[:], ident[:]),
              reads=("ident",), writes=("ident_bf",))
        kb.op("gpsimd", lambda g: g.iota(iota32[:], pattern=[[1, 32]], base=0,
                                         channel_multiplier=0,
                                         allow_small_or_imprecise_dtypes=True),
              writes=("iota32",))
        kb.dma("sync", "misc", lambda e: e.dma_start(out=nw_row, in_=d_nw),
               writes=("nw_row",))
        kb.op("vector", lambda v: v.memset(ones1, 1.0), writes=("ones1",))
        kb.op("vector", lambda v: v.memset(econst[:, 0:1], 1e-5), writes=("eps5",))
        kb.op("vector", lambda v: v.memset(econst[:, 1:2], 1e-6), writes=("eps6",))
        for c in range(4):
            cs = slice(c * 512, (c + 1) * 512)
            kb.op("tensor",
                  (lambda cs: lambda t: t.matmul(ps[0][:, 0:512], ones1,
                                                 nw_row[:, cs], start=True, stop=True))(cs),
                  reads=("ones1", "nw_row"), writes=("ps0",))
            kb.op("vector",
                  (lambda cs: lambda v: v.tensor_copy(w_bc[:, cs], ps[0][:, 0:512]))(cs),
                  reads=("ps0",), writes=("w_bc",))
        zt = V(WK, 0, 2048)
        kb.op("vector", lambda v: v.memset(zt[:], 0.0), writes=("ztile",))
        for r in range(NSLOT // 128):
            kb.dma("scalar", "xzero",
                   (lambda r: lambda e: e.dma_start(out=d_X[r * 128:(r + 1) * 128, :], in_=zt[:]))(r),
                   reads=("ztile",), writes=(f"Xdram:{r // (CAP // 64)}",))

        xbuf = V(WK, 0, 4096, F32)
        sq = V(WK, 4096, 4096, F32)
        xnbf = [V(WK, 8192, 2048), V(WK, 10240, 2048)]
        xzero_bar = kb.barrier_ev("ztile", *[f"Xdram:{e}" for e in range(NEXP)])

        for i in range(NTT):
            xr = "xb"
            kb.dma("sync", "xin",
                   (lambda i: lambda e: e.dma_start(out=xbuf[:], in_=d_x[i * 128:(i + 1) * 128, :]))(i),
                   deps=(xzero_bar if i == 0 else ()),
                   writes=(xr,))
            kb.op("vector", (lambda i: lambda v: v.reduce_sum(C(i, 0), xbuf[:], axis=AX.X))(i),
                  reads=(xr,), writes=(f"s1:{i}",))
            kb.op("scalar", (lambda i: lambda s: s.activation(sq[:], xbuf[:], AF.Square,
                                                              accum_out=C(i, 1)))(i),
                  reads=(xr,), writes=("sq", f"s2:{i}"))
            kb.op("vector", (lambda i: lambda v: v.tensor_scalar_mul(C(i, 2), C(i, 0), -1.0 / H))(i),
                  reads=(f"s1:{i}",), writes=(f"negmu:{i}",))
            kb.op("vector", (lambda i: lambda v: v.tensor_mul(C(i, 5), C(i, 2), C(i, 2)))(i),
                  reads=(f"negmu:{i}",), writes=(f"tmp:{i}",))
            kb.op("vector", (lambda i: lambda v: v.scalar_tensor_tensor(
                C(i, 5), C(i, 1), 1.0 / H, C(i, 5), ALU.mult, ALU.subtract))(i),
                  reads=(f"s2:{i}", f"tmp:{i}"), writes=(f"var:{i}",))
            kb.op("scalar", (lambda i: lambda s: s.activation(C(i, 0), C(i, 5), AF.Ln,
                                                              bias=econst[:, 0:1]))(i),
                  reads=(f"var:{i}", "eps5"), writes=(f"lt:{i}",))
            kb.op("scalar", (lambda i: lambda s: s.activation(C(i, 3), C(i, 0), AF.Exp,
                                                              scale=-0.5))(i),
                  reads=(f"lt:{i}",), writes=(f"rstd:{i}",))
            kb.op("scalar", (lambda i: lambda s: s.activation(C(i, 1), C(i, 0), AF.Exp,
                                                              scale=-1.0))(i),
                  reads=(f"lt:{i}",), writes=(f"rstd2:{i}",))
            kb.op("vector", (lambda i: lambda v: v.tensor_mul(C(i, 1), C(i, 5), C(i, 1)))(i),
                  reads=(f"var:{i}", f"rstd2:{i}"), writes=(f"q:{i}",))
            kb.op("scalar", (lambda i: lambda s: s.activation(C(i, 1), C(i, 1), AF.Ln,
                                                              bias=econst[:, 1:2]))(i),
                  reads=(f"q:{i}", "eps6"), writes=(f"lq:{i}",))
            kb.op("scalar", (lambda i: lambda s: s.activation(C(i, 1), C(i, 1), AF.Exp,
                                                              scale=-0.5))(i),
                  reads=(f"lq:{i}",), writes=(f"beta:{i}",))
            xbf = xnbf[i % 2]
            xbfr = f"xnbf{i % 2}"
            if OWN0 <= i < OWN0 + NOT_:
                o = i - OWN0
                asl = acc[:, o * H:(o + 1) * H]
                kb.op("vector", (lambda i, asl: lambda v: v.tensor_scalar(
                    asl, xbuf[:], C(i, 2), C(i, 3), ALU.add, ALU.mult))(i, asl),
                      reads=(xr, f"negmu:{i}", f"rstd:{i}"), writes=(f"acc:{o}",))
                kb.op("vector", (lambda xbf, asl: lambda v: v.tensor_copy(xbf[:], asl))(xbf, asl),
                      reads=(f"acc:{o}",), writes=(xbfr,))
            else:
                kb.op("vector", (lambda i, xbf: lambda v: v.tensor_scalar(
                    xbf[:], xbuf[:], C(i, 2), C(i, 3), ALU.add, ALU.mult))(i, xbf),
                      reads=(xr, f"negmu:{i}", f"rstd:{i}"), writes=(xbfr,))
            for kk in range(HK):
                pst, pr = (psT0, "psT0") if kk % 2 == 0 else (psT1, "psT1")
                kb.op("tensor", (lambda xbf, kk, pst: lambda t: t.transpose(
                    pst[:], xbf[:, kk * 128:(kk + 1) * 128], ident_bf[:]))(xbf, kk, pst),
                      reads=(xbfr, "ident_bf"), writes=(pr,))
                dst = ARQ[:, kk * THAL + i * 128: kk * THAL + (i + 1) * 128]
                kb.op("vector", (lambda dst, pst: lambda v: v.tensor_copy(dst, pst[:]))(dst, pst),
                      reads=(pr,), writes=(f"xnT:{kk}:{i}",))

        if PHASES < 1.5:
            _finish(kb, nc, stack)
            return nc

        hfb = kb.barrier_ev("xb", "xnbf0", "xnbf1")
        h_f32 = V(WK, 0, 4096, F32)
        h_hi = V(WK, 8192, 2048)
        h_lo = V(WK, 10240, 2048)
        for o in range(NOT_):
            i = o + OWN0
            kb.op("vector", (lambda o, i: lambda v: v.scalar_tensor_tensor(
                h_f32[:], acc[:, o * H:(o + 1) * H], C(i, 1), w_bc[:], ALU.mult, ALU.mult))(o, i),
                  deps=(hfb if o == 0 else ()),
                  reads=(f"acc:{o}", f"beta:{i}", "w_bc"), writes=("h_f32",))
            kb.op("vector", lambda v: v.tensor_copy(h_hi[:], h_f32[:]),
                  reads=("h_f32",), writes=("h_hi",))
            kb.op("vector", lambda v: v.tensor_tensor(h_lo[:], h_f32[:], h_hi[:], ALU.subtract),
                  reads=("h_f32", "h_hi"), writes=("h_lo",))
            for kk in range(HK):
                pst, pr = (psT0, "psT0") if kk % 2 == 0 else (psT1, "psT1")
                kb.op("tensor", (lambda kk, pst: lambda t: t.transpose(
                    pst[:], h_hi[:, kk * 128:(kk + 1) * 128], ident_bf[:]))(kk, pst),
                      reads=("h_hi", "ident_bf"), writes=(pr,))
                dst = ARH[:, kk * TOK + o * 128: kk * TOK + (o + 1) * 128]
                kb.op("vector", (lambda dst, pst: lambda v: v.tensor_copy(dst, pst[:]))(dst, pst),
                      reads=(pr,), writes=(f"hT:{kk}:{o}",))
            for kk in range(HK):
                pst, pr = (psT0, "psT0") if kk % 2 == 0 else (psT1, "psT1")
                kb.op("tensor", (lambda kk, pst: lambda t: t.transpose(
                    pst[:], h_lo[:, kk * 128:(kk + 1) * 128], ident_bf[:]))(kk, pst),
                      reads=("h_lo", "ident_bf"), writes=(pr,))
                dst = AR1[:, kk * TOK + o * 128: kk * TOK + (o + 1) * 128]
                kb.op("vector", (lambda dst, pst: lambda v: v.tensor_copy(dst, pst[:]))(dst, pst),
                      reads=(pr,), writes=(f"hTlo:{kk}:{o}",))

        gw_f32 = V(WK, 8192, 1024, F32)
        kb.dma("gpsimd", "wload",
               lambda g: g.dma_start(out=gw_f32[:].rearrange("p (kk e) -> p kk e", kk=HK),
                                     in_=d_gwt.rearrange("(kk p) e -> p kk e", p=128)),
               deps=kb.barrier_ev("h_hi"),
               writes=("gw_f32",))
        kb.op("vector", lambda v: v.tensor_copy(gw_hi[:], gw_f32[:]),
              reads=("gw_f32",), writes=("gw_hi",))
        kb.op("vector", lambda v: v.tensor_tensor(gw_lo[:], gw_f32[:], gw_hi[:], ALU.subtract),
              reads=("gw_f32", "gw_hi"), writes=("gw_lo",))

        AT = V(WK, 0, 2048, F32, 0, 32)
        cum = V(WK, 2048, 4096 - 2048, F32, 0, 32)
        zeros32 = V(WK, 4096, 2048, F32, 0, 32)
        h_scat = V(WK, 6144, 2048)
        h_scat_f = V(WK, 8192, 4096, F32)
        wkclaim = kb.barrier_ev("h_f32", "h_hi", "h_lo", "sq", "xb")
        kb.op("vector", lambda v: v.memset(zeros32[:], 0.0), deps=wkclaim, writes=("zeros32",))

        for o in range(NOT_):
            def gate_mm(o):
                def f(t):
                    last = None
                    n = 0
                    for kk in range(HK):
                        hi = ARH[:, kk * TOK + o * 128: kk * TOK + (o + 1) * 128]
                        lo = AR1[:, kk * TOK + o * 128: kk * TOK + (o + 1) * 128]
                        ghi = gw_hi[:, kk * 32:(kk + 1) * 32]
                        glo = gw_lo[:, kk * 32:(kk + 1) * 32]
                        for (l, r) in ((hi, ghi), (hi, glo), (lo, ghi)):
                            last = t.matmul(ps[1][:, 0:32], l, r,
                                            start=(n == 0), stop=(n == 3 * HK - 1))
                            n += 1
                    return last
                return f
            kb.op("tensor", gate_mm(o),
                  reads=tuple(f"hT:{kk}:{o}" for kk in range(HK)) +
                        tuple(f"hTlo:{kk}:{o}" for kk in range(HK)) + ("gw_hi", "gw_lo"),
                  writes=("ps1",))
            kb.op("vector", lambda v: v.tensor_copy(logit_t[:], ps[1][:, 0:32]),
                  reads=("ps1",), writes=("logit_t",))
            kb.op("vector", lambda v: v.max(maxv8[:], logit_t[:]),
                  reads=("logit_t",), writes=("maxv8",))
            kb.op("vector", lambda v: v.max_index(maxi8[:], maxv8[:], logit_t[:]),
                  reads=("maxv8", "logit_t"), writes=("maxi8",))
            tv = topv[:, o * 4:(o + 1) * 4]
            kb.op("vector", (lambda tv: lambda v: v.tensor_copy(tv, maxv8[:, 0:4]))(tv),
                  reads=("maxv8",), writes=(f"topv:{o}",))
            ef = efc[:, o * 4:(o + 1) * 4]
            kb.op("vector", (lambda ef: lambda v: v.tensor_copy(ef, maxi8[:, 0:4]))(ef),
                  reads=("maxi8",), writes=(f"ef:{o}",))
            kb.op("vector", (lambda tv: lambda v: v.tensor_scalar_mul(posc[:, 0:1], tv[:, 0:1], -1.0))(tv),
                  reads=(f"topv:{o}",), writes=("negv0",))
            wr = wrt[:, o * 4:(o + 1) * 4]
            kb.op("scalar", (lambda tv, wr: lambda s: s.activation(
                wr, tv, AF.Exp, bias=posc[:, 0:1], accum_out=posc[:, 1:2]))(tv, wr),
                  reads=(f"topv:{o}", "negv0"), writes=(f"ew:{o}", "den1"))
            kb.op("vector", lambda v: v.reciprocal(posc[:, 1:2], posc[:, 1:2]),
                  reads=("den1",), writes=("den1",))
            kb.op("vector", (lambda wr: lambda v: v.tensor_scalar_mul(wr, wr, posc[:, 1:2]))(wr),
                  reads=(f"ew:{o}", "den1"), writes=(f"wrt:{o}",))
            for k in range(4):
                efk = efc[:, o * 4 + k: o * 4 + k + 1]
                if k == 0:
                    kb.op("vector", (lambda efk: lambda v: v.tensor_scalar(
                        A_t[:], iota32[:], efk, None, ALU.is_equal))(efk),
                          reads=(f"ef:{o}", "iota32"), writes=("A_t",))
                else:
                    kb.op("vector", (lambda efk: lambda v: v.tensor_scalar(
                        oh_t[:], iota32[:], efk, None, ALU.is_equal))(efk),
                          reads=(f"ef:{o}", "iota32"), writes=("oh_t",))
                    kb.op("vector", lambda v: v.tensor_add(A_t[:], A_t[:], oh_t[:]),
                          reads=("A_t", "oh_t"), writes=("A_t",))
            kb.op("tensor", lambda t: t.transpose(psTf[0:32, 0:128], A_t[:], ident[:]),
                  reads=("A_t", "ident"), writes=("psTf",))
            dstT = AT[0:32, o * 128:(o + 1) * 128]
            kb.op("vector", (lambda dstT: lambda v: v.tensor_copy(dstT, psTf[0:32, 0:128]))(dstT),
                  deps=(wkclaim if o == 0 else ()),
                  reads=("psTf",), writes=(f"ATc:{o}",))

        kb.op("vector", lambda v: v.tensor_tensor_scan(cum[:], AT[:], zeros32[:], 0.0,
                                                       ALU.add, ALU.add),
              reads=tuple(f"ATc:{o}" for o in range(NOT_)) + ("zeros32",),
              writes=("cum",))

        bndreg = nc.gpsimd.alloc_register("bnd")
        kb.op("gpsimd", lambda g: g.reg_mov(bndreg, NSLOT - 1), writes=("bndreg",))
        hscat_bar = kb.barrier_ev("h_hi", "h_lo", "sq", "gw_f32")
        for o in range(NOT_):
            i = o + OWN0
            kb.op("tensor", (lambda o: lambda t: t.transpose(
                psTf[:, 0:32], cum[0:32, o * 128:(o + 1) * 128], ident[0:32, 0:32]))(o),
                  reads=("cum", "ident"), writes=("psTf",))
            kb.op("vector", lambda v: v.tensor_copy(pre_t[:], psTf[:, 0:32]),
                  reads=("psTf",), writes=("pre_t",))
            for k in range(4):
                col = o * 4 + k
                efk = efc[:, col:col + 1]
                kb.op("vector", (lambda efk: lambda v: v.tensor_scalar(
                    oh_t[:], iota32[:], efk, None, ALU.is_equal))(efk),
                      reads=(f"ef:{o}", "iota32"), writes=("oh_t",))
                kb.op("vector", lambda v: v.tensor_mul(oh_t[:], oh_t[:], pre_t[:]),
                      reads=("oh_t", "pre_t"), writes=("oh_t",))
                pk = posc[:, 0:1]
                kb.op("vector", lambda v: v.reduce_sum(pk, oh_t[:], axis=AX.X),
                      reads=("oh_t",), writes=("posk",))
                kb.op("vector", lambda v: v.tensor_scalar_add(pk, pk, -1.0),
                      reads=("posk",), writes=("posk",))
                kb.op("vector", (lambda efk, pk: lambda v: v.scalar_tensor_tensor(
                    addrf[:], efk, float(CAP), pk, ALU.mult, ALU.add))(efk, pk),
                      reads=(f"ef:{o}", "posk"), writes=("addrf",))
                kb.op("vector", (lambda pk: lambda v: v.tensor_scalar(
                    ovf[:], pk, float(CAP), None, ALU.is_ge))(pk),
                      reads=("posk",), writes=("ovf",))
                kb.op("vector", lambda v: v.scalar_tensor_tensor(
                    addrf[:], ovf[:], 1e6, addrf[:], ALU.mult, ALU.add),
                      reads=("ovf", "addrf"), writes=("addrf",))
                au = addrU[:, col:col + 1]
                kb.op("vector", (lambda au: lambda v: v.tensor_copy(au, addrf[:]))(au),
                      reads=("addrf",), writes=(f"addrU:{o}:{k}",))
            kb.op("vector", (lambda o, i: lambda v: v.scalar_tensor_tensor(
                h_scat_f[:], acc[:, o * H:(o + 1) * H], C(i, 1), w_bc[:], ALU.mult, ALU.mult))(o, i),
                  deps=(hscat_bar if o == 0 else ()),
                  reads=(f"acc:{o}", f"beta:{i}", "w_bc"), writes=("h_scat_f",))
            kb.op("vector", lambda v: v.tensor_copy(h_scat[:], h_scat_f[:]),
                  reads=("h_scat_f",), writes=("h_scat",))
            for k in range(4):
                col = o * 4 + k
                au = addrU[:, col:col + 1]
                kb.dma("gpsimd", "scat",
                       (lambda au: lambda g: g.indirect_dma_start(
                           out=d_X, out_offset=IOX(au, 0),
                           in_=h_scat[:], in_offset=None,
                           bounds_check=bndreg, oob_is_err=False))(au),
                       reads=(f"addrU:{o}:{k}", "h_scat", "bndreg") +
                             tuple(f"Xdram:{e}" for e in range(NEXP)),
                       writes=(f"Xscat:{o}:{k}",))

        if PHASES < 2:
            _finish(kb, nc, stack)
            return nc

        kb.dma("gpsimd", "wload",
               lambda g: g.dma_start(
                   out=masks_bf[:].rearrange("p (o k) -> p o k", o=NOT_),
                   in_=d_mask.rearrange("(o p) k -> p o k", p=128)),
               deps=kb.barrier_ev("w_bc"),
               writes=("masks",))

        wqh = V(WK, 0, 2048)
        wkh = V(WK, 2048, 2048)
        wvh = V(WK, 4096, 2048)
        QTh = V(WK, 6144, 1024)
        KTh = V(WK, 7168, 1280)
        VTh = V(WK, 8448, 1280)
        V_h = V(WK, 9728, 1280)
        smx0b = V(WK, 11008, 768, F32)
        P_bfb = V(WK, 11776, 384)
        den4b = V(MS, 4980, 8, F32)
        p2claim = kb.barrier_ev("cum", "zeros32", "h_scat", "h_scat_f", "gw_f32",
                                "gw_hi", "gw_lo", "h_f32", "h_hi", "h_lo",
                                *[f"ATc:{o}" for o in range(NOT_)])
        smx_claim = kb.barrier_ev("ones1")

        def wslice_load(dst, dsrc, h):
            return lambda g: g.dma_start(
                out=dst[:].rearrange("p (kk m) -> p kk m", kk=HK),
                in_=dsrc[h])

        kb.dma("gpsimd", "wload", wslice_load(wqh, d_wq, 0), deps=p2claim, writes=("wqh",))
        kb.dma("gpsimd", "wload", wslice_load(wkh, d_wk, 0), deps=p2claim, writes=("wkh",))
        kb.dma("gpsimd", "wload", wslice_load(wvh, d_wv, 0), deps=p2claim, writes=("wvh",))
        for h in range(NHEAD):

            def proj(dstname, dst, wbuf, wname, tok0, chunks, scaled):
                for ci, (c0, w) in enumerate(chunks):
                    bank = ps[ci % 2]
                    bname = f"ps{ci % 2}"

                    def mmg(wbuf=wbuf, c0=c0, w=w, bank=bank):
                        def f(t):
                            last = None
                            for kk in range(HK):
                                last = t.matmul(
                                    bank[:, 0:w],
                                    wbuf[:, kk * DH:(kk + 1) * DH],
                                    ARQ[:, kk * THAL + tok0 + c0: kk * THAL + tok0 + c0 + w],
                                    start=(kk == 0), stop=(kk == HK - 1))
                            return last
                        return f
                    kb.op("tensor", mmg(),
                          reads=(wname,) + tuple(f"xnT:{kk}:{ii}" for kk in range(HK)
                                                 for ii in range(NTT)),
                          writes=(bname,))
                    dsl = dst[:, c0:c0 + w]
                    if scaled:
                        kb.op("scalar", (lambda dsl, bank, w: lambda s: s.mul(
                            dsl, bank[:, 0:w], SCALE))(dsl, bank, w),
                              reads=(bname,), writes=(dstname,))
                    else:
                        kb.op("vector", (lambda dsl, bank, w: lambda v: v.tensor_copy(
                            dsl, bank[:, 0:w]))(dsl, bank, w),
                              reads=(bname,), writes=(dstname,))

            proj("QTh", QTh, wqh, "wqh", HALO, [(0, 512), (512, 512)], True)
            if h + 1 < NHEAD:
                kb.dma("gpsimd", "wload", wslice_load(wqh, d_wq, h + 1), writes=("wqh",))
            proj("KTh", KTh, wkh, "wkh", 0, [(0, 512), (512, 512), (1024, 256)], False)
            if h + 1 < NHEAD:
                kb.dma("gpsimd", "wload", wslice_load(wkh, d_wk, h + 1), writes=("wkh",))
            proj("VTh", VTh, wvh, "wvh", 0, [(0, 512), (512, 512), (1024, 256)], False)
            if h + 1 < NHEAD:
                kb.dma("gpsimd", "wload", wslice_load(wvh, d_wv, h + 1), writes=("wvh",))
            for c in range(NTT):
                pst, pr = (psT0, "psT0") if c % 2 == 0 else (psT1, "psT1")
                kb.op("tensor", (lambda c, pst: lambda t: t.transpose(
                    pst[:], VTh[:, c * 128:(c + 1) * 128], ident_bf[:]))(c, pst),
                      reads=("VTh", "ident_bf"), writes=(pr,))
                kb.op("vector", (lambda c, pst: lambda v: v.tensor_copy(
                    V_h[:, c * 128:(c + 1) * 128], pst[:]))(c, pst),
                      reads=(pr,), writes=("V_h",))

            for qt in range(NOT_):
                par = qt % 2
                smx = smx0 if par == 0 else smx0b
                smxn = f"smx{par}"
                Pb = P_bf if par == 0 else P_bfb
                Pbn = f"P_bf{par}"
                dn = den4 if par == 0 else den4b
                dnn = f"den{par}"
                avb, avbn = (ps[4], "ps4") if par == 0 else (psTf, "psTf")
                sbank = ps[2 + par]
                sbname = f"ps{2 + par}"
                kb.op("tensor", (lambda qt, sbank: lambda t: t.matmul(
                    sbank[:, 0:KW], QTh[:, qt * 128:(qt + 1) * 128],
                    KTh[:, qt * 128: qt * 128 + KW], start=True, stop=True))(qt, sbank),
                      reads=("QTh", "KTh"), writes=(sbname,))
                kb.op("vector", (lambda qt, sbank, smx: lambda v: v.tensor_tensor(
                    smx[:], sbank[:, 0:KW], masks_bf[:, qt * KW:(qt + 1) * KW],
                    ALU.add))(qt, sbank, smx),
                      deps=(smx_claim if (h == 0 and qt <= 1) else ()),
                      reads=(sbname, "masks"), writes=(smxn,))
                kb.op("vector", (lambda smx, dn: lambda v: v.reduce_max(
                    dn[:, 0:1], smx[:], axis=AX.X))(smx, dn),
                      reads=(smxn,), writes=(f"rmax{par}",))
                kb.op("vector", (lambda dn: lambda v: v.tensor_scalar_mul(
                    dn[:, 1:2], dn[:, 0:1], -1.0))(dn),
                      reads=(f"rmax{par}",), writes=(f"nmax{par}",))
                kb.op("scalar", (lambda smx, dn, Pb: lambda s: s.activation(
                    Pb[:], smx[:], AF.Exp, bias=dn[:, 1:2], accum_out=dn[:, 2:3]))(smx, dn, Pb),
                      reads=(smxn, f"nmax{par}"), writes=(Pbn, f"den{par}"))
                kb.op("vector", (lambda dn: lambda v: v.reciprocal(
                    dn[:, 3:4], dn[:, 2:3]))(dn),
                      reads=(f"den{par}",), writes=(f"rden{par}",))
                kb.op("vector", (lambda Pb, dn: lambda v: v.tensor_scalar_mul(
                    Pb[:], Pb[:], dn[:, 3:4]))(Pb, dn),
                      reads=(Pbn, f"rden{par}"), writes=(Pbn,))
                for c in range(3):
                    pst, pr = (psT0, "psT0") if c % 2 == 0 else (psT1, "psT1")
                    kb.op("tensor", (lambda c, pst, Pb: lambda t: t.transpose(
                        pst[:], Pb[:, c * 128:(c + 1) * 128], ident_bf[:]))(c, pst, Pb),
                          reads=(Pbn, "ident_bf"), writes=(pr,))
                    kb.op("vector", (lambda c, pst: lambda v: v.tensor_copy(
                        PT_bf[:, c * 128:(c + 1) * 128], pst[:]))(c, pst),
                          reads=(pr,), writes=("PT_bf",))

                def avmm(qt, avb):
                    def f(t):
                        last = None
                        for c in range(3):
                            last = t.matmul(avb[:, 0:128],
                                            V_h[:, (qt + c) * 128:(qt + c + 1) * 128],
                                            PT_bf[:, c * 128:(c + 1) * 128],
                                            start=(c == 0), stop=(c == 2))
                        return last
                    return f
                kb.op("tensor", avmm(qt, avb), reads=("V_h", "PT_bf"), writes=(avbn,))
                dsl = AR1[:, h * TOK + qt * 128:(h * TOK + (qt + 1) * 128)]
                kb.op("vector", (lambda dsl, avb: lambda v: v.tensor_copy(
                    dsl, avb[:, 0:128]))(dsl, avb),
                      deps=(kb.barrier_ev(*[f"hTlo:{kk}:{o}" for kk in range(HK)
                                            for o in range(NOT_)])
                            if (h == 0 and qt == 0) else ()),
                      reads=(avbn,), writes=(f"OT:{h}:{qt}",))


        if PHASES < 3:
            _finish(kb, nc, stack)
            return nc

        sg_sl = V(ARQ, 0, 8192)
        su_sl = V(ARQ, 8192, 8192)
        sd_sl = V(ARQ, 16384, 4096)
        sact = V(WK, 0, 8192)
        scr3 = V(WK, 8192, 1024, F32)
        arq_claim = kb.barrier_ev(*[f"xnT:{kk}:{ii}" for kk in range(HK) for ii in range(NTT)])
        wk3_claim = kb.barrier_ev("wqh", "wkh", "wvh", "QTh", "KTh", "VTh", "V_h")

        def slice_load(dst, dsrc, n, kks):
            return lambda g: g.dma_start(
                out=dst[:].rearrange("p (kk c) -> p kk c", kk=kks),
                in_=dsrc[n])

        for n in range(2):
            kb.dma("gpsimd", "wload", slice_load(sg_sl, d_sg, n, HK),
                   deps=(arq_claim if n == 0 else ()), writes=("sg_sl",))
            kb.dma("gpsimd", "wload", slice_load(su_sl, d_su, n, HK),
                   deps=(arq_claim if n == 0 else ()), writes=("su_sl",))
            for o in range(NOT_):
                def shmm(sl, o):
                    def f(t):
                        last = None
                        for kk in range(HK):
                            last = t.matmul(ps[0 if sl is sg_sl else 1][:, 0:512],
                                            ARH[:, kk * TOK + o * 128: kk * TOK + (o + 1) * 128],
                                            sl[:, kk * 512:(kk + 1) * 512],
                                            start=(kk == 0), stop=(kk == HK - 1))
                        return last
                    return f
                kb.op("tensor", shmm(sg_sl, o),
                      reads=("sg_sl",) + tuple(f"hT:{kk}:{o}" for kk in range(HK)),
                      writes=("ps0",))
                kb.op("tensor", shmm(su_sl, o),
                      reads=("su_sl",) + tuple(f"hT:{kk}:{o}" for kk in range(HK)),
                      writes=("ps1",))
                kb.op("scalar", lambda s: s.activation(scr3[:], ps[0][:, 0:512], AF.Silu),
                      deps=(wk3_claim if (n == 0 and o == 0) else ()),
                      reads=("ps0",), writes=("scr3",))
                dsl = sact[:, o * 1024 + n * 512: o * 1024 + (n + 1) * 512]
                kb.op("vector", (lambda dsl: lambda v: v.tensor_mul(
                    dsl, scr3[:], ps[1][:, 0:512]))(dsl),
                      reads=("scr3", "ps1"), writes=(f"sact:{o}:{n}",))

        sactT = V(ARQ, 0, 8192)
        sT_claim = kb.barrier_ev("sg_sl", "su_sl")
        for o in range(NOT_):
            for kk in range(8):
                pst, pr = (psT0, "psT0") if kk % 2 == 0 else (psT1, "psT1")
                src = sact[:, o * 1024 + kk * 128: o * 1024 + (kk + 1) * 128]
                kb.op("tensor", (lambda src, pst: lambda t: t.transpose(
                    pst[:], src, ident_bf[:]))(src, pst),
                      reads=(f"sact:{o}:{kk // 4}", "ident_bf"), writes=(pr,))
                dsl = sactT[:, kk * TOK + o * 128: kk * TOK + (o + 1) * 128]
                kb.op("vector", (lambda dsl, pst: lambda v: v.tensor_copy(dsl, pst[:]))(dsl, pst),
                      deps=(sT_claim if (o == 0 and kk == 0) else ()),
                      reads=(pr,), writes=(f"sactT:{kk}:{o}",))

        for n in range(4):
            kb.dma("gpsimd", "wload", slice_load(sd_sl, d_sd, n, 8),
                   deps=(sT_claim if n == 0 else ()), writes=("sd_sl",))
            for o in range(NOT_):
                bank = ps[2 + o % 2]
                bname = f"ps{2 + o % 2}"

                def dnmm(o, n, bank):
                    def f(t):
                        last = None
                        for kk in range(8):
                            last = t.matmul(bank[:, 0:512],
                                            sactT[:, kk * TOK + o * 128: kk * TOK + (o + 1) * 128],
                                            sd_sl[:, kk * 512:(kk + 1) * 512],
                                            start=(kk == 0), stop=(kk == 7))
                        return last
                    return f
                kb.op("tensor", dnmm(o, n, bank),
                      reads=("sd_sl",) + tuple(f"sactT:{kk}:{o}" for kk in range(8)),
                      writes=(bname,))
                asl = acc[:, o * H + n * 512: o * H + (n + 1) * 512]
                kb.op("vector", (lambda asl, bank: lambda v: v.tensor_tensor(
                    asl, asl, bank[:, 0:512], ALU.add))(asl, bank),
                      reads=(bname, f"acc:{o}"), writes=(f"acc:{o}",))

        if PHASES < 4:
            _finish(kb, nc, stack)
            return nc

        xst = [(V(ARH, 0, 2048), V(ARH, 2048, 2048, BF16, 0, 64)),
               (V(ARH, 4096, 2048), V(ARH, 6144, 2048, BF16, 0, 64))]
        XT = [V(ARH, 8192, 3072), V(ARH, 11264, 3072)]
        W13H = [V(ARQ, i * 4096, 4096) for i in range(4)]
        W2sl = [V(ARQ, 16384, 4096), V(WK, 5632, 4096)]
        scrG = V(WK, 0, 1024, F32)
        actb = [V(WK, 1024, 1024), V(WK, 2048, 1024)]
        actTb = [V(WK, 3072, 768), V(WK, 3840, 768)]
        Ystc = [V(WK, 4608, 512), V(WK, 5120, 512)]

        arh_claim = kb.barrier_ev(*[f"hT:{kk}:{o}" for kk in range(HK) for o in range(NOT_)])
        arq4_claim = kb.barrier_ev("sd_sl",
                                   *[f"sactT:{kk}:{o}" for kk in range(8) for o in range(NOT_)])
        wk4_claim = kb.barrier_ev("sact", "scr3",
                                  *[f"sact:{o}:{n}" for o in range(NOT_) for n in range(2)])
        wk4b_claim = kb.barrier_ev("sact", "scr3",
                                   *[f"sactT:{kk}:{o}" for kk in range(8) for o in range(NOT_)])
        xdeps = tuple(f"Xscat:{o}:{k}" for o in range(NOT_) for k in range(4)) + \
                tuple(f"Xdram:{e}" for e in range(16))

        pending_w2 = []

        def emit_w2_load(e, n):
            w2b = W2sl[n % 2]
            kb.dma("gpsimd", "wload",
                   (lambda e, n, w2b: lambda g: g.dma_start(
                       out=w2b[:].rearrange("p (kk c) -> p kk c", kk=8),
                       in_=d_w2[e, n]))(e, n, w2b),
                   deps=(arq4_claim if (e == 0 and n == 0) else
                         (wk4b_claim if (e == 0 and n == 1) else ())),
                   writes=(f"W2sl:{n % 2}",))

        for e in range(NEXP):
            eb = e % 2
            x0, x1 = xst[eb]
            xt = XT[eb]
            kb.dma("scalar", "xload",
                   (lambda e, x0: lambda s: s.dma_start(
                       out=x0[:], in_=d_X[e * CAP: e * CAP + 128, :]))(e, x0),
                   deps=(arh_claim if e < 2 else ()),
                   reads=xdeps, writes=(f"x0:{eb}",))
            kb.dma("scalar", "xload",
                   (lambda e, x1: lambda s: s.dma_start(
                       out=x1[:], in_=d_X[e * CAP + 128: (e + 1) * CAP, :]))(e, x1),
                   deps=(arh_claim if e < 2 else ()),
                   reads=xdeps, writes=(f"x1:{eb}",))
            for kk in range(HK):
                pst, pr = (psT0, "psT0") if kk % 2 == 0 else (psT1, "psT1")
                kb.op("tensor", (lambda x0, kk, pst: lambda t: t.transpose(
                    pst[:], x0[:, kk * 128:(kk + 1) * 128], ident_bf[:]))(x0, kk, pst),
                      reads=(f"x0:{eb}", "ident_bf"), writes=(pr,))
                dsl = xt[:, kk * CAP: kk * CAP + 128]
                kb.op("vector", (lambda dsl, pst: lambda v: v.tensor_copy(dsl, pst[:]))(dsl, pst),
                      reads=(pr,), writes=(f"XT:{eb}",))
                pst2, pr2 = (psT1, "psT1") if kk % 2 == 0 else (psT0, "psT0")
                kb.op("tensor", (lambda x1, kk, pst2: lambda t: t.transpose(
                    pst2[0:128, 0:64], x1[:, kk * 128:(kk + 1) * 128],
                    ident_bf[0:64, 0:64]))(x1, kk, pst2),
                      reads=(f"x1:{eb}", "ident_bf"), writes=(pr2,))
                dsl2 = xt[:, kk * CAP + 128: (kk + 1) * CAP]
                kb.op("vector", (lambda dsl2, pst2: lambda v: v.tensor_copy(
                    dsl2, pst2[0:128, 0:64]))(dsl2, pst2),
                      reads=(pr2,), writes=(f"XT:{eb}",))

            for pi, pair in enumerate(((0, 2), (1, 3))):
                bidx = {pair[0]: 0, pair[1]: 2}
                for half in range(2):
                    for ni, n in enumerate(pair):
                        buf = (pi * 4 + half * 2 + ni) % 4
                        kb.dma("gpsimd", "wload",
                               (lambda e, n, half, wb: lambda g: g.dma_start(
                                   out=wb[:].rearrange("p (kk c) -> p kk c", kk=8),
                                   in_=d_w13[e, n][:, half * 8:(half + 1) * 8, :]))(
                                       e, n, half, W13H[buf]),
                               deps=(arq4_claim if (e == 0 and pi == 0 and half == 0) else ()),
                               writes=(f"W13H:{buf}",))
                for half in range(2):
                    for ni, n in enumerate(pair):
                        buf = (pi * 4 + half * 2 + ni) % 4
                        for st in range(2):
                            rows = 128 if st == 0 else 64
                            bank = ps[bidx[n] + st]

                            def w13mm(half, st, rows, bank, buf, xt=xt):
                                def f(t):
                                    last = None
                                    for kk8 in range(8):
                                        kk = half * 8 + kk8
                                        if st == 0:
                                            l = xt[:, kk * CAP: kk * CAP + 128]
                                        else:
                                            l = xt[:, kk * CAP + 128: (kk + 1) * CAP]
                                        last = t.matmul(
                                            bank[0:rows, 0:512], l,
                                            W13H[buf][:, kk8 * 512:(kk8 + 1) * 512],
                                            start=(kk == 0), stop=(kk == HK - 1))
                                    return last
                                return f
                            kb.op("tensor", w13mm(half, st, rows, bank, buf),
                                  reads=(f"XT:{eb}", f"W13H:{buf}"),
                                  writes=(f"ps{bidx[n] + st}",))
                for st in range(2):
                    rows = 128 if st == 0 else 64
                    kb.op("scalar", (lambda st, rows, b=ps[bidx[pair[0]] + st]: lambda s: s.activation(
                        scrG[0:rows, :], b[0:rows, 0:512], AF.Silu))(st, rows),
                          deps=(wk4_claim if (e == 0 and pi == 0 and st == 0) else ()),
                          reads=(f"ps{bidx[pair[0]] + st}",), writes=("scrG",))
                    ab = actb[pi]
                    dsl = ab[0:rows, (0 if st == 0 else 512):(512 if st == 0 else 1024)]
                    kb.op("vector", (lambda dsl, rows, b=ps[bidx[pair[1]] + st]: lambda v: v.tensor_mul(
                        dsl, scrG[0:rows, :], b[0:rows, 0:512]))(dsl, rows),
                          reads=("scrG", f"ps{bidx[pair[1]] + st}"),
                          writes=(f"act:{pi}",))
            for kk in range(8):
                pi = kk // 4
                ab = actb[pi]
                atb = actTb[pi]
                base = (kk % 4) * 192
                lc = (kk % 4) * 128
                pst, pr = (psT0, "psT0") if kk % 2 == 0 else (psT1, "psT1")
                kb.op("tensor", (lambda ab, lc, pst: lambda t: t.transpose(
                    pst[:], ab[0:128, lc:lc + 128], ident_bf[:]))(ab, lc, pst),
                      reads=(f"act:{pi}", "ident_bf"), writes=(pr,))
                kb.op("vector", (lambda atb, base, pst: lambda v: v.tensor_copy(
                    atb[:, base:base + 128], pst[:]))(atb, base, pst),
                      reads=(pr,), writes=(f"actT:{pi}",))
                pst2, pr2 = (psT1, "psT1") if kk % 2 == 0 else (psT0, "psT0")
                kb.op("tensor", (lambda ab, lc, pst2: lambda t: t.transpose(
                    pst2[0:128, 0:64], ab[0:64, 512 + lc: 512 + lc + 128],
                    ident_bf[0:64, 0:64]))(ab, lc, pst2),
                      reads=(f"act:{pi}", "ident_bf"), writes=(pr2,))
                kb.op("vector", (lambda atb, base, pst2: lambda v: v.tensor_copy(
                    atb[:, base + 128:base + 192], pst2[0:128, 0:64]))(atb, base, pst2),
                      reads=(pr2,), writes=(f"actT:{pi}",))

            emit_w2_load(e, 0)
            emit_w2_load(e, 1)
            for n in range(4):
                w2b = W2sl[n % 2]
                if n >= 2:
                    emit_w2_load(e, n)
                for st in range(2):
                    rows = 128 if st == 0 else 64

                    def w2mm(n, st, rows, w2b):
                        def f(t):
                            last = None
                            for kk in range(8):
                                atb = actTb[kk // 4]
                                base = (kk % 4) * 192
                                l = atb[:, base:base + 128] if st == 0 else \
                                    atb[:, base + 128:base + 192]
                                last = t.matmul(ps[4][0:rows, 0:512], l,
                                                w2b[:, kk * 512:(kk + 1) * 512],
                                                start=(kk == 0), stop=(kk == 7))
                            return last
                        return f
                    kb.op("tensor", w2mm(n, st, rows, w2b),
                          reads=("actT:0", "actT:1", f"W2sl:{n % 2}"), writes=("ps4",))
                    yc = Ystc[n % 2]
                    dsl = yc[0:rows, 0:512]
                    kb.op("vector", (lambda dsl, rows: lambda v: v.tensor_copy(
                        dsl, ps[4][0:rows, 0:512]))(dsl, rows),
                          reads=("ps4",), writes=(f"Ystc:{n % 2}",))
                    if st == 0:
                        dd = d_Y[e * CAP: e * CAP + 128, n * 512:(n + 1) * 512]
                    else:
                        dd = d_Y[e * CAP + 128: (e + 1) * CAP, n * 512:(n + 1) * 512]
                    kb.dma("scalar", "ystore",
                           (lambda dd, yc, rows: lambda s: s.dma_start(
                               out=dd, in_=yc[0:rows, 0:512]))(dd, yc, rows),
                           reads=(f"Ystc:{n % 2}",), writes=(f"Ydram:{e}",))

        if PHASES < 5:
            _finish(kb, nc, stack)
            return nc

        gtl = [V(ARH, 0, 2048), V(ARH, 2048, 2048)]
        gclaim = kb.barrier_ev("XT:0", "XT:1", "x0:0", "x0:1", "x1:0", "x1:1")
        ydeps = tuple(f"Ydram:{e}" for e in range(NEXP))
        for o in range(NOT_):
            for k in range(4):
                col = o * 4 + k
                gb = (o * 4 + k) % 2
                gt = gtl[gb]
                kb.op("vector", (lambda gt: lambda v: v.memset(gt[:], 0.0))(gt),
                      deps=(gclaim if col < 2 else ()),
                      writes=(f"gt:{gb}",))
                kb.dma("gpsimd", "gath",
                       (lambda gt, col: lambda g: g.indirect_dma_start(
                           out=gt[:], out_offset=None,
                           in_=d_Y, in_offset=IOX(addrU[:, col:col + 1], 0),
                           bounds_check=bndreg, oob_is_err=False))(gt, col),
                       reads=(f"addrU:{o}:{k}", f"gt:{gb}", "bndreg") + ydeps,
                       writes=(f"gt:{gb}",))
                asl = acc[:, o * H:(o + 1) * H]
                kb.op("vector", (lambda asl, gt, col: lambda v: v.scalar_tensor_tensor(
                    asl, gt[:], wrt[:, col:col + 1], asl, ALU.mult, ALU.add))(asl, gt, col),
                      reads=(f"gt:{gb}", f"wrt:{o}", f"acc:{o}"),
                      writes=(f"acc:{o}",))

        if PHASES < 6:
            _finish(kb, nc, stack)
            return nc

        Wosl = [V(ARQ, 0, 8192), V(ARQ, 8192, 8192)]
        xch = [V(WK, 0, 1024, F32), V(WK, 1024, 1024, F32)]
        gscr = V(WK, 2048, 1024, F32)
        m1 = V(WK, 3072, 1024, F32)
        outc = [V(WK, 4096, 1024, F32), V(WK, 5120, 1024, F32)]
        wk6_claim = kb.barrier_ev("scrG", "act:0", "act:1", "actT:0", "actT:1",
                                  "Ystc:0", "Ystc:1", "W2sl:0", "W2sl:1")
        arq6_claim = kb.barrier_ev("W13H:0", "W13H:1", "W13H:2", "W13H:3", "W2sl:0")

        for n in range(4):
            wb = Wosl[n % 2]
            wbn = f"Wosl:{n % 2}"
            kb.dma("gpsimd", "wload",
                   (lambda n, wb: lambda g: g.dma_start(
                       out=wb[:].rearrange("p (kk c) -> p kk c", kk=HK),
                       in_=d_wo[n]))(n, wb),
                   deps=(arq6_claim if n < 2 else ()),
                   writes=(wbn,))
            for o in range(NOT_):
                xb = xch[o % 2]
                kb.dma("sync", "xin2",
                       (lambda o, n, xb: lambda e: e.dma_start(
                           out=xb[:],
                           in_=d_x[(o + 1) * 128:(o + 2) * 128, n * 512:(n + 1) * 512]))(o, n, xb),
                       deps=(wk6_claim if (n == 0 and o < 2) else ()),
                       writes=(f"xch:{o % 2}",))
                bank = ps[o % 2]
                bname = f"ps{o % 2}"

                def womm(o, n, bank, wb):
                    def f(t):
                        last = None
                        for kk in range(HK):
                            last = t.matmul(bank[:, 0:512],
                                            AR1[:, kk * TOK + o * 128: kk * TOK + (o + 1) * 128],
                                            wb[:, kk * 512:(kk + 1) * 512],
                                            start=(kk == 0), stop=(kk == HK - 1))
                        return last
                    return f
                kb.op("tensor", womm(o, n, bank, wb),
                      reads=(wbn,) + tuple(f"OT:{kk}:{o}" for kk in range(HK)),
                      writes=(bname,))
                kb.op("scalar", (lambda o, n: lambda s: s.activation(
                    gscr[:], acc[:, o * H + n * 512: o * H + (n + 1) * 512], AF.Silu))(o, n),
                      deps=(wk6_claim if (n == 0 and o == 0) else ()),
                      reads=(f"acc:{o}",), writes=("gscr",))
                kb.op("vector", (lambda bank: lambda v: v.tensor_mul(
                    m1[:], bank[:, 0:512], gscr[:]))(bank),
                      reads=(bname, "gscr"), writes=("m1",))
                oc = outc[o % 2]
                kb.op("vector", (lambda oc, xb: lambda v: v.tensor_add(
                    oc[:], xb[:], m1[:]))(oc, xb),
                      reads=("m1", f"xch:{o % 2}"), writes=(f"outc:{o % 2}",))
                kb.dma("sync", "out",
                       (lambda o, n, oc: lambda e: e.dma_start(
                           out=d_out[o * 128:(o + 1) * 128, n * 512:(n + 1) * 512],
                           in_=oc[:]))(o, n, oc),
                       reads=(f"outc:{o % 2}",), writes=())

        _finish(kb, nc, stack)
    return nc


def _finish(kb, nc, stack):
    allev = kb.all_current_events()
    for name in DEBUG:
        apobj = kb.named[name]
        shape = [apobj.shape[0], apobj.shape[1]]
        dt = apobj.dtype
        dout = nc.dram_tensor("dbg_" + name, shape, dt, kind="ExternalOutput").ap()
        kb.dma("sync", "dbg",
               (lambda dout, apobj: lambda e: e.dma_start(out=dout, in_=apobj[:]))(dout, apobj),
               deps=allev)
    kb.finalize_and_emit(stack.enter_context(nc.Block()))


def _host_masks():
    masks = np.zeros((NCORES, NOT_, 128, KW), dtype=np.float32)
    NEG = -1e30
    for c in range(NCORES):
        seg = c % 4
        for qt in range(NOT_):
            gq0 = seg * TOK + qt * 128
            k0 = gq0 - HALO
            for qr2 in range(2):
                qb = (gq0 // 64) + qr2
                for kc in range(KW // 64):
                    kbk = (k0 // 64) + kc
                    ok = (abs(kbk - qb) <= 2) and (0 <= kbk * 64 < 4096)
                    if not ok:
                        masks[c, qt, qr2 * 64:(qr2 + 1) * 64, kc * 64:(kc + 1) * 64] = NEG
    return masks


def kernel(L0_attn_hidden_states, L0_attn_Wq, L0_attn_Wk, L0_attn_Wv, L0_attn_Wo,
           L0_ffn_norm_w, L0_ffn_gate_w, L0_ffn_w13, L0_ffn_w2,
           L0_ffn_shared_gate_w, L0_ffn_shared_up_w, L0_ffn_shared_down_w):
    x = np.ascontiguousarray(np.asarray(L0_attn_hidden_states, dtype=np.float32))
    B, S, _ = x.shape
    masks = _host_masks()

    C = np.ascontiguousarray

    def tile_cols(w, kk, nn):
        return C(np.asarray(w, np.float32).reshape(kk, 128, nn, 512).transpose(2, 1, 0, 3))

    wq = np.asarray(L0_attn_Wq, np.float32).reshape(HK, 128, NHEAD, DH).transpose(2, 1, 0, 3)
    wk = np.asarray(L0_attn_Wk, np.float32).reshape(HK, 128, NHEAD, DH).transpose(2, 1, 0, 3)
    wv = np.asarray(L0_attn_Wv, np.float32).reshape(HK, 128, NHEAD, DH).transpose(2, 1, 0, 3)
    w13 = np.asarray(L0_ffn_w13, np.float32).reshape(NEXP, HK, 128, 4, 512).transpose(0, 3, 2, 1, 4)
    w2 = np.asarray(L0_ffn_w2, np.float32).reshape(NEXP, 8, 128, 4, 512).transpose(0, 3, 2, 1, 4)
    weights = {
        "Wq": C(wq), "Wk": C(wk), "Wv": C(wv),
        "Wo": tile_cols(L0_attn_Wo, HK, 4),
        "norm_w": C(np.asarray(L0_ffn_norm_w, np.float32)).reshape(1, H),
        "gate_wT": C(np.asarray(L0_ffn_gate_w, np.float32).T),
        "w13": C(w13), "w2": C(w2),
        "sgw": tile_cols(L0_ffn_shared_gate_w, HK, 2),
        "suw": tile_cols(L0_ffn_shared_up_w, HK, 2),
        "sdw": tile_cols(L0_ffn_shared_down_w, 8, 4),
    }

    in_maps = []
    for c in range(NCORES):
        b, seg = c // 4, c % 4
        lo = seg * TOK - HALO
        xh = np.zeros((THAL, H), dtype=np.float32)
        s0 = max(lo, 0)
        s1 = min(lo + THAL, S)
        xh[s0 - lo: s1 - lo, :] = x[b, s0:s1, :]
        m = {"x_halo": xh, "masks": np.ascontiguousarray(masks[c].reshape(NOT_ * 128, KW))}
        m.update(weights)
        in_maps.append(m)

    nc = build_program()
    res = run_bass_kernel_spmd(nc, in_maps, core_ids=list(range(NCORES)))

    out = np.empty((B, S, H), dtype=np.float32)
    for c in range(NCORES):
        b, seg = c // 4, c % 4
        out[b, seg * TOK:(seg + 1) * TOK, :] = np.asarray(res.results[c]["out"], dtype=np.float32)
    if DEBUG:
        kernel.debug_results = res.results
    return out


# version 06 (final)
# speedup vs baseline: 1.0181x; 1.0181x over previous
import numpy as np
from contextlib import ExitStack

import concourse.bass as bass
import concourse.mybir as mybir
from concourse.bass_utils import run_bass_kernel_spmd

F32 = mybir.dt.float32
BF16 = mybir.dt.bfloat16
U32 = mybir.dt.uint32
AF = mybir.ActivationFunctionType
ALU = mybir.AluOpType
AX = mybir.AxisListType
IOX = bass.IndirectOffsetOnAxis

NCORES = 8
H = 2048
HK = 16
TOK = 1024
HALO = 128
THAL = 1280
NTT = 10
OWN0 = 1
NOT_ = 8
NHEAD = 16
DH = 128
NEXP = 32
CAP = 192
NSLOT = NEXP * CAP
II = 1024
KW = 384
SCALE = float(DH) ** -0.5

DEBUG = []
PHASES = 99


class Ev:
    __slots__ = ("sem", "val", "needed", "inc")
    def __init__(self, sem, inc):
        self.sem = sem
        self.val = None
        self.needed = False
        self.inc = inc


class KB:
    ENGINES = ("tensor", "vector", "scalar", "gpsimd", "sync")

    def __init__(self, nc, stack):
        self.nc = nc
        self.stack = stack
        self.sems = {}
        self.queue = {e: [] for e in self.ENGINES}
        self.sem_ops = {}
        self.eng_sem = {}
        for e in self.ENGINES:
            n = "E_" + e
            self._mksem(n)
            self.eng_sem[e] = n
        self.res = {}
        self.named = {}

    def _mksem(self, name):
        self.sems[name] = self.stack.enter_context(self.nc.semaphore(name))
        self.sem_ops[name] = []

    def chan(self, name):
        n = "C_" + name
        if n not in self.sems:
            self._mksem(n)
        return n

    def _collect(self, deps, reads, writes):
        evs = list(deps)
        for r in reads:
            e = self.res.get(r)
            if e and e[0] is not None:
                evs.append(e[0])
        for w in writes:
            e = self.res.get(w)
            if e:
                if e[0] is not None:
                    evs.append(e[0])
                evs.extend(e[1])
        for e in evs:
            e.needed = True
        return evs

    def _post(self, ev, reads, writes):
        for w in writes:
            self.res[w] = [ev, []]
        for r in reads:
            self.res.setdefault(r, [None, []])[1].append(ev)

    def op(self, engine, fn, deps=(), reads=(), writes=()):
        evs = self._collect(deps, reads, writes)
        ev = Ev(self.eng_sem[engine], 1)
        self.sem_ops[ev.sem].append(ev)
        self.queue[engine].append((evs, fn, ev))
        self._post(ev, reads, writes)
        return ev

    def dma(self, engine, channel, fn, deps=(), reads=(), writes=()):
        evs = self._collect(deps, reads, writes)
        ev = Ev(self.chan(channel), 16)
        ev.needed = True
        self.sem_ops[ev.sem].append(ev)
        self.queue[engine].append((evs, fn, ev))
        self._post(ev, reads, writes)
        return ev

    def barrier_ev(self, *names):
        evs = []
        for n in names:
            e = self.res.get(n)
            if e:
                if e[0] is not None:
                    evs.append(e[0])
                evs.extend(e[1])
        return evs

    def all_current_events(self):
        return [lst[-1] for lst in self.sem_ops.values() if lst]

    def finalize_and_emit(self, block, final_engine="sync"):
        for sem, evs in self.sem_ops.items():
            cum = 0
            for ev in evs:
                if ev.needed:
                    cum += ev.inc
                    ev.val = cum
        finals = []
        for sem, evs in self.sem_ops.items():
            last = None
            for ev in evs:
                if ev.needed:
                    last = ev
            if last is not None:
                finals.append(last)
        kb = self

        def make_runner(engine):
            ops = kb.queue[engine]
            is_final = engine == final_engine

            def run(eng):
                hw = {}
                for (evs, fn, ev) in ops:
                    todo = {}
                    for e in evs:
                        if e.val is None:
                            continue
                        if hw.get(e.sem, 0) >= e.val:
                            continue
                        if todo.get(e.sem, 0) < e.val:
                            todo[e.sem] = e.val
                    for s, v in sorted(todo.items()):
                        eng.wait_ge(kb.sems[s], v)
                        hw[s] = v
                    inst = fn(eng)
                    if ev.val is not None:
                        inst.then_inc(kb.sems[ev.sem], ev.inc)
                if is_final:
                    for e in finals:
                        if e.sem != kb.eng_sem[engine]:
                            eng.wait_ge(kb.sems[e.sem], e.val)
            return run

        for engine in self.ENGINES:
            getattr(block, engine)(make_runner(engine))


def build_program():
    nc = bass.Bass("TRN2", target_bir_lowering=False)

    d_x = nc.dram_tensor("x_halo", [THAL, H], F32, kind="ExternalInput").ap()
    d_mask = nc.dram_tensor("masks", [NOT_ * 128, KW], F32, kind="ExternalInput").ap()
    d_wq = nc.dram_tensor("Wq", [NHEAD, 128, HK, DH], F32, kind="ExternalInput").ap()
    d_wk = nc.dram_tensor("Wk", [NHEAD, 128, HK, DH], F32, kind="ExternalInput").ap()
    d_wv = nc.dram_tensor("Wv", [NHEAD, 128, HK, DH], F32, kind="ExternalInput").ap()
    d_wo = nc.dram_tensor("Wo", [4, 128, HK, 512], F32, kind="ExternalInput").ap()
    d_nw = nc.dram_tensor("norm_w", [1, H], F32, kind="ExternalInput").ap()
    d_gwt = nc.dram_tensor("gate_wT", [H, NEXP], F32, kind="ExternalInput").ap()
    d_w13 = nc.dram_tensor("w13", [NEXP, 4, 128, HK, 512], F32, kind="ExternalInput").ap()
    d_w2 = nc.dram_tensor("w2", [NEXP, 4, 128, 8, 512], F32, kind="ExternalInput").ap()
    d_sg = nc.dram_tensor("sgw", [2, 128, HK, 512], F32, kind="ExternalInput").ap()
    d_su = nc.dram_tensor("suw", [2, 128, HK, 512], F32, kind="ExternalInput").ap()
    d_sd = nc.dram_tensor("sdw", [4, 128, 8, 512], F32, kind="ExternalInput").ap()
    d_out = nc.dram_tensor("out", [TOK, H], F32, kind="ExternalOutput").ap()
    d_X = nc.dram_tensor("Xbuf", [NSLOT, H], BF16, kind="Internal").ap()
    d_Y = nc.dram_tensor("Ybuf", [NSLOT, H], BF16, kind="Internal").ap()


    with ExitStack() as stack:
        kb = KB(nc, stack)
        sb = lambda name, shape, dt: stack.enter_context(nc.sbuf_tensor(name, shape, dt))
        psm = lambda name, shape, dt: stack.enter_context(nc.psum_tensor(name, shape, dt))

        acc = sb("acc", [128, NOT_ * H], F32)
        ARQ = sb("ARQ", [128, 20480], BF16)
        ARH = sb("ARH", [128, 16384], BF16)
        AR1 = sb("AR1", [128, 16384], BF16)
        WK = sb("WK", [128, 12288], BF16)
        MS = sb("MS", [128, 7808], BF16)

        def V(arena, off, n, dt=BF16, p0=0, p1=128):
            a = arena[p0:p1, off:off + n]
            return a.bitcast(dt) if dt != BF16 else a

        w_bc = V(MS, 0, 4096, F32)
        masks_bf = V(MS, 0, 3072)
        nw_row = V(MS, 0, 4096, F32, 0, 1)
        ident = V(MS, 4096, 256, F32)
        ident_bf = V(MS, 4352, 128)
        scal = V(MS, 4480, 128, F32)
        econst = V(MS, 4608, 8, F32)
        iota32 = V(MS, 4616, 64, F32)
        topv = V(MS, 4680, 64, F32)
        wrt = V(MS, 4744, 64, F32)
        posc = V(MS, 4808, 4, F32)
        addrf = V(MS, 4812, 2, F32)
        ovf = V(MS, 4816, 2, F32)
        addrU = V(MS, 4820, 64, U32)
        efc = V(MS, 4884, 64, F32)
        maxv8 = V(MS, 4948, 16, F32)
        maxi8 = V(MS, 4964, 16, U32)
        logit_t = V(MS, 4980, 64, F32)
        A_t = V(MS, 5044, 64, F32)
        oh_t = V(MS, 5108, 64, F32)
        pre_t = V(MS, 5172, 64, F32)
        gw_hi = V(MS, 5236, 512)
        gw_lo = V(MS, 5748, 512)
        smx0 = V(MS, 6260, 768, F32)
        ones1 = V(MS, 6260, 256, F32, 0, 1)
        den4 = V(MS, 7028, 8, F32)
        P_bf = V(MS, 7036, 384)
        PT_bf = V(MS, 7420, 384)

        ps = [psm(f"ps{i}", [128, 512], F32) for i in range(5)]
        psT0 = psm("psT0", [128, 128], BF16)
        psT1 = psm("psT1", [128, 128], BF16)
        psTf = psm("psTf", [128, 128], F32)

        kb.named.update(acc=acc, ARQ=ARQ, ARH=ARH, AR1=AR1, WK=WK, MS=MS, Xbuf=d_X, Ybuf=d_Y)

        C = lambda i, j: scal[:, i * 6 + j: i * 6 + j + 1]

        kb.op("gpsimd", lambda g: g.iota(ident[:], pattern=[[1, 128]], base=0,
                                         channel_multiplier=-1,
                                         allow_small_or_imprecise_dtypes=True),
              writes=("ident",))
        kb.op("gpsimd", lambda g: g.tensor_single_scalar(ident[:], ident[:], 0.0, ALU.is_equal),
              reads=("ident",), writes=("ident",))
        kb.op("vector", lambda v: v.tensor_copy(ident_bf[:], ident[:]),
              reads=("ident",), writes=("ident_bf",))
        kb.op("gpsimd", lambda g: g.iota(iota32[:], pattern=[[1, 32]], base=0,
                                         channel_multiplier=0,
                                         allow_small_or_imprecise_dtypes=True),
              writes=("iota32",))
        kb.dma("sync", "misc", lambda e: e.dma_start(out=nw_row, in_=d_nw),
               writes=("nw_row",))
        kb.op("vector", lambda v: v.memset(ones1, 1.0), writes=("ones1",))
        kb.op("vector", lambda v: v.memset(econst[:, 0:1], 1e-5), writes=("eps5",))
        kb.op("vector", lambda v: v.memset(econst[:, 1:2], 1e-6), writes=("eps6",))
        for c in range(4):
            cs = slice(c * 512, (c + 1) * 512)
            kb.op("tensor",
                  (lambda cs: lambda t: t.matmul(ps[0][:, 0:512], ones1,
                                                 nw_row[:, cs], start=True, stop=True))(cs),
                  reads=("ones1", "nw_row"), writes=("ps0",))
            kb.op("vector",
                  (lambda cs: lambda v: v.tensor_copy(w_bc[:, cs], ps[0][:, 0:512]))(cs),
                  reads=("ps0",), writes=("w_bc",))
        zt = V(WK, 0, 2048)
        kb.op("vector", lambda v: v.memset(zt[:], 0.0), writes=("ztile",))
        for r in range(NSLOT // 128):
            kb.dma("scalar", "xzero",
                   (lambda r: lambda e: e.dma_start(out=d_X[r * 128:(r + 1) * 128, :], in_=zt[:]))(r),
                   reads=("ztile",), writes=(f"Xdram:{r // (CAP // 64)}",))

        xbuf = V(WK, 0, 4096, F32)
        sq = V(WK, 4096, 4096, F32)
        xnbf = [V(WK, 8192, 2048), V(WK, 10240, 2048)]
        xzero_bar = kb.barrier_ev("ztile", *[f"Xdram:{e}" for e in range(NEXP)])

        for i in range(NTT):
            xr = "xb"
            kb.dma("sync", "xin",
                   (lambda i: lambda e: e.dma_start(out=xbuf[:], in_=d_x[i * 128:(i + 1) * 128, :]))(i),
                   deps=(xzero_bar if i == 0 else ()),
                   writes=(xr,))
            kb.op("vector", (lambda i: lambda v: v.reduce_sum(C(i, 0), xbuf[:], axis=AX.X))(i),
                  reads=(xr,), writes=(f"s1:{i}",))
            kb.op("scalar", (lambda i: lambda s: s.activation(sq[:], xbuf[:], AF.Square,
                                                              accum_out=C(i, 1)))(i),
                  reads=(xr,), writes=("sq", f"s2:{i}"))
            kb.op("vector", (lambda i: lambda v: v.tensor_scalar_mul(C(i, 2), C(i, 0), -1.0 / H))(i),
                  reads=(f"s1:{i}",), writes=(f"negmu:{i}",))
            kb.op("vector", (lambda i: lambda v: v.tensor_mul(C(i, 5), C(i, 2), C(i, 2)))(i),
                  reads=(f"negmu:{i}",), writes=(f"tmp:{i}",))
            kb.op("vector", (lambda i: lambda v: v.scalar_tensor_tensor(
                C(i, 5), C(i, 1), 1.0 / H, C(i, 5), ALU.mult, ALU.subtract))(i),
                  reads=(f"s2:{i}", f"tmp:{i}"), writes=(f"var:{i}",))
            kb.op("scalar", (lambda i: lambda s: s.activation(C(i, 0), C(i, 5), AF.Ln,
                                                              bias=econst[:, 0:1]))(i),
                  reads=(f"var:{i}", "eps5"), writes=(f"lt:{i}",))
            kb.op("scalar", (lambda i: lambda s: s.activation(C(i, 3), C(i, 0), AF.Exp,
                                                              scale=-0.5))(i),
                  reads=(f"lt:{i}",), writes=(f"rstd:{i}",))
            kb.op("scalar", (lambda i: lambda s: s.activation(C(i, 1), C(i, 0), AF.Exp,
                                                              scale=-1.0))(i),
                  reads=(f"lt:{i}",), writes=(f"rstd2:{i}",))
            kb.op("vector", (lambda i: lambda v: v.tensor_mul(C(i, 1), C(i, 5), C(i, 1)))(i),
                  reads=(f"var:{i}", f"rstd2:{i}"), writes=(f"q:{i}",))
            kb.op("scalar", (lambda i: lambda s: s.activation(C(i, 1), C(i, 1), AF.Ln,
                                                              bias=econst[:, 1:2]))(i),
                  reads=(f"q:{i}", "eps6"), writes=(f"lq:{i}",))
            kb.op("scalar", (lambda i: lambda s: s.activation(C(i, 1), C(i, 1), AF.Exp,
                                                              scale=-0.5))(i),
                  reads=(f"lq:{i}",), writes=(f"beta:{i}",))
            xbf = xnbf[i % 2]
            xbfr = f"xnbf{i % 2}"
            if OWN0 <= i < OWN0 + NOT_:
                o = i - OWN0
                asl = acc[:, o * H:(o + 1) * H]
                kb.op("vector", (lambda i, asl: lambda v: v.tensor_scalar(
                    asl, xbuf[:], C(i, 2), C(i, 3), ALU.add, ALU.mult))(i, asl),
                      reads=(xr, f"negmu:{i}", f"rstd:{i}"), writes=(f"acc:{o}",))
                kb.op("vector", (lambda xbf, asl: lambda v: v.tensor_copy(xbf[:], asl))(xbf, asl),
                      reads=(f"acc:{o}",), writes=(xbfr,))
            else:
                kb.op("vector", (lambda i, xbf: lambda v: v.tensor_scalar(
                    xbf[:], xbuf[:], C(i, 2), C(i, 3), ALU.add, ALU.mult))(i, xbf),
                      reads=(xr, f"negmu:{i}", f"rstd:{i}"), writes=(xbfr,))
            for kk in range(HK):
                pst, pr = (psT0, "psT0") if kk % 2 == 0 else (psT1, "psT1")
                kb.op("tensor", (lambda xbf, kk, pst: lambda t: t.transpose(
                    pst[:], xbf[:, kk * 128:(kk + 1) * 128], ident_bf[:]))(xbf, kk, pst),
                      reads=(xbfr, "ident_bf"), writes=(pr,))
                dst = ARQ[:, kk * THAL + i * 128: kk * THAL + (i + 1) * 128]
                kb.op("vector", (lambda dst, pst: lambda v: v.tensor_copy(dst, pst[:]))(dst, pst),
                      reads=(pr,), writes=(f"xnT:{kk}:{i}",))

        if PHASES < 1.5:
            _finish(kb, nc, stack)
            return nc

        hfb = kb.barrier_ev("xb", "xnbf0", "xnbf1")
        h_f32 = V(WK, 0, 4096, F32)
        h_hi = V(WK, 8192, 2048)
        h_lo = V(WK, 10240, 2048)
        for o in range(NOT_):
            i = o + OWN0
            kb.op("vector", (lambda o, i: lambda v: v.scalar_tensor_tensor(
                h_f32[:], acc[:, o * H:(o + 1) * H], C(i, 1), w_bc[:], ALU.mult, ALU.mult))(o, i),
                  deps=(hfb if o == 0 else ()),
                  reads=(f"acc:{o}", f"beta:{i}", "w_bc"), writes=("h_f32",))
            kb.op("vector", lambda v: v.tensor_copy(h_hi[:], h_f32[:]),
                  reads=("h_f32",), writes=("h_hi",))
            kb.op("vector", lambda v: v.tensor_tensor(h_lo[:], h_f32[:], h_hi[:], ALU.subtract),
                  reads=("h_f32", "h_hi"), writes=("h_lo",))
            for kk in range(HK):
                pst, pr = (psT0, "psT0") if kk % 2 == 0 else (psT1, "psT1")
                kb.op("tensor", (lambda kk, pst: lambda t: t.transpose(
                    pst[:], h_hi[:, kk * 128:(kk + 1) * 128], ident_bf[:]))(kk, pst),
                      reads=("h_hi", "ident_bf"), writes=(pr,))
                dst = ARH[:, kk * TOK + o * 128: kk * TOK + (o + 1) * 128]
                kb.op("vector", (lambda dst, pst: lambda v: v.tensor_copy(dst, pst[:]))(dst, pst),
                      reads=(pr,), writes=(f"hT:{kk}:{o}",))
            for kk in range(HK):
                pst, pr = (psT0, "psT0") if kk % 2 == 0 else (psT1, "psT1")
                kb.op("tensor", (lambda kk, pst: lambda t: t.transpose(
                    pst[:], h_lo[:, kk * 128:(kk + 1) * 128], ident_bf[:]))(kk, pst),
                      reads=("h_lo", "ident_bf"), writes=(pr,))
                dst = AR1[:, kk * TOK + o * 128: kk * TOK + (o + 1) * 128]
                kb.op("vector", (lambda dst, pst: lambda v: v.tensor_copy(dst, pst[:]))(dst, pst),
                      reads=(pr,), writes=(f"hTlo:{kk}:{o}",))

        gw_f32 = V(WK, 8192, 1024, F32)
        kb.dma("gpsimd", "wload",
               lambda g: g.dma_start(out=gw_f32[:].rearrange("p (kk e) -> p kk e", kk=HK),
                                     in_=d_gwt.rearrange("(kk p) e -> p kk e", p=128)),
               deps=kb.barrier_ev("h_hi"),
               writes=("gw_f32",))
        kb.op("vector", lambda v: v.tensor_copy(gw_hi[:], gw_f32[:]),
              reads=("gw_f32",), writes=("gw_hi",))
        kb.op("vector", lambda v: v.tensor_tensor(gw_lo[:], gw_f32[:], gw_hi[:], ALU.subtract),
              reads=("gw_f32", "gw_hi"), writes=("gw_lo",))

        AT = V(WK, 0, 2048, F32, 0, 32)
        cum = V(WK, 2048, 4096 - 2048, F32, 0, 32)
        zeros32 = V(WK, 4096, 2048, F32, 0, 32)
        h_scat = V(WK, 6144, 2048)
        h_scat_f = V(WK, 8192, 4096, F32)
        wkclaim = kb.barrier_ev("h_f32", "h_hi", "h_lo", "sq", "xb")
        kb.op("vector", lambda v: v.memset(zeros32[:], 0.0), deps=wkclaim, writes=("zeros32",))

        for o in range(NOT_):
            def gate_mm(o):
                def f(t):
                    last = None
                    n = 0
                    for kk in range(HK):
                        hi = ARH[:, kk * TOK + o * 128: kk * TOK + (o + 1) * 128]
                        lo = AR1[:, kk * TOK + o * 128: kk * TOK + (o + 1) * 128]
                        ghi = gw_hi[:, kk * 32:(kk + 1) * 32]
                        glo = gw_lo[:, kk * 32:(kk + 1) * 32]
                        for (l, r) in ((hi, ghi), (hi, glo), (lo, ghi)):
                            last = t.matmul(ps[1][:, 0:32], l, r,
                                            start=(n == 0), stop=(n == 3 * HK - 1))
                            n += 1
                    return last
                return f
            kb.op("tensor", gate_mm(o),
                  reads=tuple(f"hT:{kk}:{o}" for kk in range(HK)) +
                        tuple(f"hTlo:{kk}:{o}" for kk in range(HK)) + ("gw_hi", "gw_lo"),
                  writes=("ps1",))
            kb.op("vector", lambda v: v.tensor_copy(logit_t[:], ps[1][:, 0:32]),
                  reads=("ps1",), writes=("logit_t",))
            kb.op("vector", lambda v: v.max(maxv8[:], logit_t[:]),
                  reads=("logit_t",), writes=("maxv8",))
            kb.op("vector", lambda v: v.max_index(maxi8[:], maxv8[:], logit_t[:]),
                  reads=("maxv8", "logit_t"), writes=("maxi8",))
            tv = topv[:, o * 4:(o + 1) * 4]
            kb.op("vector", (lambda tv: lambda v: v.tensor_copy(tv, maxv8[:, 0:4]))(tv),
                  reads=("maxv8",), writes=(f"topv:{o}",))
            ef = efc[:, o * 4:(o + 1) * 4]
            kb.op("vector", (lambda ef: lambda v: v.tensor_copy(ef, maxi8[:, 0:4]))(ef),
                  reads=("maxi8",), writes=(f"ef:{o}",))
            kb.op("vector", (lambda tv: lambda v: v.tensor_scalar_mul(posc[:, 0:1], tv[:, 0:1], -1.0))(tv),
                  reads=(f"topv:{o}",), writes=("negv0",))
            wr = wrt[:, o * 4:(o + 1) * 4]
            kb.op("scalar", (lambda tv, wr: lambda s: s.activation(
                wr, tv, AF.Exp, bias=posc[:, 0:1], accum_out=posc[:, 1:2]))(tv, wr),
                  reads=(f"topv:{o}", "negv0"), writes=(f"ew:{o}", "den1"))
            kb.op("vector", lambda v: v.reciprocal(posc[:, 1:2], posc[:, 1:2]),
                  reads=("den1",), writes=("den1",))
            kb.op("vector", (lambda wr: lambda v: v.tensor_scalar_mul(wr, wr, posc[:, 1:2]))(wr),
                  reads=(f"ew:{o}", "den1"), writes=(f"wrt:{o}",))
            for k in range(4):
                efk = efc[:, o * 4 + k: o * 4 + k + 1]
                if k == 0:
                    kb.op("vector", (lambda efk: lambda v: v.tensor_scalar(
                        A_t[:], iota32[:], efk, None, ALU.is_equal))(efk),
                          reads=(f"ef:{o}", "iota32"), writes=("A_t",))
                else:
                    kb.op("vector", (lambda efk: lambda v: v.tensor_scalar(
                        oh_t[:], iota32[:], efk, None, ALU.is_equal))(efk),
                          reads=(f"ef:{o}", "iota32"), writes=("oh_t",))
                    kb.op("vector", lambda v: v.tensor_add(A_t[:], A_t[:], oh_t[:]),
                          reads=("A_t", "oh_t"), writes=("A_t",))
            kb.op("tensor", lambda t: t.transpose(psTf[0:32, 0:128], A_t[:], ident[:]),
                  reads=("A_t", "ident"), writes=("psTf",))
            dstT = AT[0:32, o * 128:(o + 1) * 128]
            kb.op("vector", (lambda dstT: lambda v: v.tensor_copy(dstT, psTf[0:32, 0:128]))(dstT),
                  deps=(wkclaim if o == 0 else ()),
                  reads=("psTf",), writes=(f"ATc:{o}",))

        kb.op("vector", lambda v: v.tensor_tensor_scan(cum[:], AT[:], zeros32[:], 0.0,
                                                       ALU.add, ALU.add),
              reads=tuple(f"ATc:{o}" for o in range(NOT_)) + ("zeros32",),
              writes=("cum",))

        bndreg = nc.gpsimd.alloc_register("bnd")
        kb.op("gpsimd", lambda g: g.reg_mov(bndreg, NSLOT - 1), writes=("bndreg",))
        hscat_bar = kb.barrier_ev("h_hi", "h_lo", "sq", "gw_f32")
        for o in range(NOT_):
            i = o + OWN0
            kb.op("tensor", (lambda o: lambda t: t.transpose(
                psTf[:, 0:32], cum[0:32, o * 128:(o + 1) * 128], ident[0:32, 0:32]))(o),
                  reads=("cum", "ident"), writes=("psTf",))
            kb.op("vector", lambda v: v.tensor_copy(pre_t[:], psTf[:, 0:32]),
                  reads=("psTf",), writes=("pre_t",))
            for k in range(4):
                col = o * 4 + k
                efk = efc[:, col:col + 1]
                kb.op("vector", (lambda efk: lambda v: v.tensor_scalar(
                    oh_t[:], iota32[:], efk, None, ALU.is_equal))(efk),
                      reads=(f"ef:{o}", "iota32"), writes=("oh_t",))
                kb.op("vector", lambda v: v.tensor_mul(oh_t[:], oh_t[:], pre_t[:]),
                      reads=("oh_t", "pre_t"), writes=("oh_t",))
                pk = posc[:, 0:1]
                kb.op("vector", lambda v: v.reduce_sum(pk, oh_t[:], axis=AX.X),
                      reads=("oh_t",), writes=("posk",))
                kb.op("vector", lambda v: v.tensor_scalar_add(pk, pk, -1.0),
                      reads=("posk",), writes=("posk",))
                kb.op("vector", (lambda efk, pk: lambda v: v.scalar_tensor_tensor(
                    addrf[:], efk, float(CAP), pk, ALU.mult, ALU.add))(efk, pk),
                      reads=(f"ef:{o}", "posk"), writes=("addrf",))
                kb.op("vector", (lambda pk: lambda v: v.tensor_scalar(
                    ovf[:], pk, float(CAP), None, ALU.is_ge))(pk),
                      reads=("posk",), writes=("ovf",))
                kb.op("vector", lambda v: v.scalar_tensor_tensor(
                    addrf[:], ovf[:], 1e6, addrf[:], ALU.mult, ALU.add),
                      reads=("ovf", "addrf"), writes=("addrf",))
                au = addrU[:, col:col + 1]
                kb.op("vector", (lambda au: lambda v: v.tensor_copy(au, addrf[:]))(au),
                      reads=("addrf",), writes=(f"addrU:{o}:{k}",))
            kb.op("vector", (lambda o, i: lambda v: v.scalar_tensor_tensor(
                h_scat_f[:], acc[:, o * H:(o + 1) * H], C(i, 1), w_bc[:], ALU.mult, ALU.mult))(o, i),
                  deps=(hscat_bar if o == 0 else ()),
                  reads=(f"acc:{o}", f"beta:{i}", "w_bc"), writes=("h_scat_f",))
            kb.op("vector", lambda v: v.tensor_copy(h_scat[:], h_scat_f[:]),
                  reads=("h_scat_f",), writes=("h_scat",))
            for k in range(4):
                col = o * 4 + k
                au = addrU[:, col:col + 1]
                kb.dma("gpsimd", "scat",
                       (lambda au: lambda g: g.indirect_dma_start(
                           out=d_X, out_offset=IOX(au, 0),
                           in_=h_scat[:], in_offset=None,
                           bounds_check=bndreg, oob_is_err=False))(au),
                       reads=(f"addrU:{o}:{k}", "h_scat", "bndreg") +
                             tuple(f"Xdram:{e}" for e in range(NEXP)),
                       writes=(f"Xscat:{o}:{k}",))

        if PHASES < 2:
            _finish(kb, nc, stack)
            return nc

        kb.dma("gpsimd", "wload",
               lambda g: g.dma_start(
                   out=masks_bf[:].rearrange("p (o k) -> p o k", o=NOT_),
                   in_=d_mask.rearrange("(o p) k -> p o k", p=128)),
               deps=kb.barrier_ev("w_bc"),
               writes=("masks",))

        wqh = V(WK, 0, 2048)
        wkh = V(WK, 2048, 2048)
        wvh = V(WK, 4096, 2048)
        QTh = V(WK, 6144, 1024)
        KTh = V(WK, 7168, 1280)
        VTh = V(WK, 8448, 1280)
        V_h = V(WK, 9728, 1280)
        smx0b = V(WK, 11008, 768, F32)
        P_bfb = V(WK, 11776, 384)
        den4b = V(MS, 4980, 8, F32)
        p2claim = kb.barrier_ev("cum", "zeros32", "h_scat", "h_scat_f", "gw_f32",
                                "gw_hi", "gw_lo", "h_f32", "h_hi", "h_lo",
                                *[f"ATc:{o}" for o in range(NOT_)])
        smx_claim = kb.barrier_ev("ones1")

        def wslice_load(dst, dsrc, h):
            return lambda g: g.dma_start(
                out=dst[:].rearrange("p (kk m) -> p kk m", kk=HK),
                in_=dsrc[h])

        kb.dma("gpsimd", "wload", wslice_load(wqh, d_wq, 0), deps=p2claim, writes=("wqh",))
        kb.dma("gpsimd", "wload", wslice_load(wkh, d_wk, 0), deps=p2claim, writes=("wkh",))
        kb.dma("gpsimd", "wload", wslice_load(wvh, d_wv, 0), deps=p2claim, writes=("wvh",))
        for h in range(NHEAD):

            def proj(dstname, dst, wbuf, wname, tok0, chunks, scaled):
                for ci, (c0, w) in enumerate(chunks):
                    bank = ps[ci % 2]
                    bname = f"ps{ci % 2}"

                    def mmg(wbuf=wbuf, c0=c0, w=w, bank=bank):
                        def f(t):
                            last = None
                            for kk in range(HK):
                                last = t.matmul(
                                    bank[:, 0:w],
                                    wbuf[:, kk * DH:(kk + 1) * DH],
                                    ARQ[:, kk * THAL + tok0 + c0: kk * THAL + tok0 + c0 + w],
                                    start=(kk == 0), stop=(kk == HK - 1))
                            return last
                        return f
                    kb.op("tensor", mmg(),
                          reads=(wname,) + tuple(f"xnT:{kk}:{ii}" for kk in range(HK)
                                                 for ii in range(NTT)),
                          writes=(bname,))
                    dsl = dst[:, c0:c0 + w]
                    if scaled:
                        kb.op("scalar", (lambda dsl, bank, w: lambda s: s.mul(
                            dsl, bank[:, 0:w], SCALE))(dsl, bank, w),
                              reads=(bname,), writes=(dstname,))
                    else:
                        kb.op("vector", (lambda dsl, bank, w: lambda v: v.tensor_copy(
                            dsl, bank[:, 0:w]))(dsl, bank, w),
                              reads=(bname,), writes=(dstname,))

            proj("QTh", QTh, wqh, "wqh", HALO, [(0, 512), (512, 512)], True)
            if h + 1 < NHEAD:
                kb.dma("gpsimd", "wload", wslice_load(wqh, d_wq, h + 1), writes=("wqh",))
            proj("KTh", KTh, wkh, "wkh", 0, [(0, 512), (512, 512), (1024, 256)], False)
            if h + 1 < NHEAD:
                kb.dma("gpsimd", "wload", wslice_load(wkh, d_wk, h + 1), writes=("wkh",))
            proj("VTh", VTh, wvh, "wvh", 0, [(0, 512), (512, 512), (1024, 256)], False)
            if h + 1 < NHEAD:
                kb.dma("gpsimd", "wload", wslice_load(wvh, d_wv, h + 1), writes=("wvh",))
            for c in range(NTT):
                pst, pr = (psT0, "psT0") if c % 2 == 0 else (psT1, "psT1")
                kb.op("tensor", (lambda c, pst: lambda t: t.transpose(
                    pst[:], VTh[:, c * 128:(c + 1) * 128], ident_bf[:]))(c, pst),
                      reads=("VTh", "ident_bf"), writes=(pr,))
                kb.op("vector", (lambda c, pst: lambda v: v.tensor_copy(
                    V_h[:, c * 128:(c + 1) * 128], pst[:]))(c, pst),
                      reads=(pr,), writes=("V_h",))

            for qt in range(NOT_):
                par = qt % 2
                smx = smx0 if par == 0 else smx0b
                smxn = f"smx{par}"
                Pb = P_bf if par == 0 else P_bfb
                Pbn = f"P_bf{par}"
                dn = den4 if par == 0 else den4b
                dnn = f"den{par}"
                avb, avbn = (ps[4], "ps4") if par == 0 else (psTf, "psTf")
                sbank = ps[2 + par]
                sbname = f"ps{2 + par}"
                kb.op("tensor", (lambda qt, sbank: lambda t: t.matmul(
                    sbank[:, 0:KW], QTh[:, qt * 128:(qt + 1) * 128],
                    KTh[:, qt * 128: qt * 128 + KW], start=True, stop=True))(qt, sbank),
                      reads=("QTh", "KTh"), writes=(sbname,))
                kb.op("vector", (lambda qt, sbank, smx: lambda v: v.tensor_tensor(
                    smx[:], sbank[:, 0:KW], masks_bf[:, qt * KW:(qt + 1) * KW],
                    ALU.add))(qt, sbank, smx),
                      deps=(smx_claim if (h == 0 and qt <= 1) else ()),
                      reads=(sbname, "masks"), writes=(smxn,))
                kb.op("vector", (lambda smx, dn: lambda v: v.reduce_max(
                    dn[:, 0:1], smx[:], axis=AX.X))(smx, dn),
                      reads=(smxn,), writes=(f"rmax{par}",))
                kb.op("vector", (lambda dn: lambda v: v.tensor_scalar_mul(
                    dn[:, 1:2], dn[:, 0:1], -1.0))(dn),
                      reads=(f"rmax{par}",), writes=(f"nmax{par}",))
                kb.op("scalar", (lambda smx, dn, Pb: lambda s: s.activation(
                    Pb[:], smx[:], AF.Exp, bias=dn[:, 1:2], accum_out=dn[:, 2:3]))(smx, dn, Pb),
                      reads=(smxn, f"nmax{par}"), writes=(Pbn, f"den{par}"))
                kb.op("vector", (lambda dn: lambda v: v.reciprocal(
                    dn[:, 3:4], dn[:, 2:3]))(dn),
                      reads=(f"den{par}",), writes=(f"rden{par}",))
                kb.op("vector", (lambda Pb, dn: lambda v: v.tensor_scalar_mul(
                    Pb[:], Pb[:], dn[:, 3:4]))(Pb, dn),
                      reads=(Pbn, f"rden{par}"), writes=(Pbn,))
                for c in range(3):
                    pst, pr = (psT0, "psT0") if c % 2 == 0 else (psT1, "psT1")
                    kb.op("tensor", (lambda c, pst, Pb: lambda t: t.transpose(
                        pst[:], Pb[:, c * 128:(c + 1) * 128], ident_bf[:]))(c, pst, Pb),
                          reads=(Pbn, "ident_bf"), writes=(pr,))
                    kb.op("vector", (lambda c, pst: lambda v: v.tensor_copy(
                        PT_bf[:, c * 128:(c + 1) * 128], pst[:]))(c, pst),
                          reads=(pr,), writes=("PT_bf",))

                def avmm(qt, avb):
                    def f(t):
                        last = None
                        for c in range(3):
                            last = t.matmul(avb[:, 0:128],
                                            V_h[:, (qt + c) * 128:(qt + c + 1) * 128],
                                            PT_bf[:, c * 128:(c + 1) * 128],
                                            start=(c == 0), stop=(c == 2))
                        return last
                    return f
                kb.op("tensor", avmm(qt, avb), reads=("V_h", "PT_bf"), writes=(avbn,))
                dsl = AR1[:, h * TOK + qt * 128:(h * TOK + (qt + 1) * 128)]
                kb.op("vector", (lambda dsl, avb: lambda v: v.tensor_copy(
                    dsl, avb[:, 0:128]))(dsl, avb),
                      deps=(kb.barrier_ev(*[f"hTlo:{kk}:{o}" for kk in range(HK)
                                            for o in range(NOT_)])
                            if (h == 0 and qt == 0) else ()),
                      reads=(avbn,), writes=(f"OT:{h}:{qt}",))


        if PHASES < 3:
            _finish(kb, nc, stack)
            return nc

        sg_sl = V(ARQ, 0, 8192)
        su_sl = V(ARQ, 8192, 8192)
        sd_sl = V(ARQ, 16384, 4096)
        sact = V(WK, 0, 8192)
        scr3 = V(WK, 8192, 1024, F32)
        arq_claim = kb.barrier_ev(*[f"xnT:{kk}:{ii}" for kk in range(HK) for ii in range(NTT)])
        wk3_claim = kb.barrier_ev("wqh", "wkh", "wvh", "QTh", "KTh", "VTh", "V_h")

        def slice_load(dst, dsrc, n, kks):
            return lambda g: g.dma_start(
                out=dst[:].rearrange("p (kk c) -> p kk c", kk=kks),
                in_=dsrc[n])

        for n in range(2):
            kb.dma("gpsimd", "wload", slice_load(sg_sl, d_sg, n, HK),
                   deps=(arq_claim if n == 0 else ()), writes=("sg_sl",))
            kb.dma("gpsimd", "wload", slice_load(su_sl, d_su, n, HK),
                   deps=(arq_claim if n == 0 else ()), writes=("su_sl",))
            for o in range(NOT_):
                def shmm(sl, o):
                    def f(t):
                        last = None
                        for kk in range(HK):
                            last = t.matmul(ps[0 if sl is sg_sl else 1][:, 0:512],
                                            ARH[:, kk * TOK + o * 128: kk * TOK + (o + 1) * 128],
                                            sl[:, kk * 512:(kk + 1) * 512],
                                            start=(kk == 0), stop=(kk == HK - 1))
                        return last
                    return f
                kb.op("tensor", shmm(sg_sl, o),
                      reads=("sg_sl",) + tuple(f"hT:{kk}:{o}" for kk in range(HK)),
                      writes=("ps0",))
                kb.op("tensor", shmm(su_sl, o),
                      reads=("su_sl",) + tuple(f"hT:{kk}:{o}" for kk in range(HK)),
                      writes=("ps1",))
                kb.op("scalar", lambda s: s.activation(scr3[:], ps[0][:, 0:512], AF.Silu),
                      deps=(wk3_claim if (n == 0 and o == 0) else ()),
                      reads=("ps0",), writes=("scr3",))
                dsl = sact[:, o * 1024 + n * 512: o * 1024 + (n + 1) * 512]
                kb.op("vector", (lambda dsl: lambda v: v.tensor_mul(
                    dsl, scr3[:], ps[1][:, 0:512]))(dsl),
                      reads=("scr3", "ps1"), writes=(f"sact:{o}:{n}",))

        sactT = V(ARQ, 0, 8192)
        sT_claim = kb.barrier_ev("sg_sl", "su_sl")
        for o in range(NOT_):
            for kk in range(8):
                pst, pr = (psT0, "psT0") if kk % 2 == 0 else (psT1, "psT1")
                src = sact[:, o * 1024 + kk * 128: o * 1024 + (kk + 1) * 128]
                kb.op("tensor", (lambda src, pst: lambda t: t.transpose(
                    pst[:], src, ident_bf[:]))(src, pst),
                      reads=(f"sact:{o}:{kk // 4}", "ident_bf"), writes=(pr,))
                dsl = sactT[:, kk * TOK + o * 128: kk * TOK + (o + 1) * 128]
                kb.op("vector", (lambda dsl, pst: lambda v: v.tensor_copy(dsl, pst[:]))(dsl, pst),
                      deps=(sT_claim if (o == 0 and kk == 0) else ()),
                      reads=(pr,), writes=(f"sactT:{kk}:{o}",))

        for n in range(4):
            kb.dma("gpsimd", "wload", slice_load(sd_sl, d_sd, n, 8),
                   deps=(sT_claim if n == 0 else ()), writes=("sd_sl",))
            for o in range(NOT_):
                bank = ps[2 + o % 2]
                bname = f"ps{2 + o % 2}"

                def dnmm(o, n, bank):
                    def f(t):
                        last = None
                        for kk in range(8):
                            last = t.matmul(bank[:, 0:512],
                                            sactT[:, kk * TOK + o * 128: kk * TOK + (o + 1) * 128],
                                            sd_sl[:, kk * 512:(kk + 1) * 512],
                                            start=(kk == 0), stop=(kk == 7))
                        return last
                    return f
                kb.op("tensor", dnmm(o, n, bank),
                      reads=("sd_sl",) + tuple(f"sactT:{kk}:{o}" for kk in range(8)),
                      writes=(bname,))
                asl = acc[:, o * H + n * 512: o * H + (n + 1) * 512]
                kb.op("vector", (lambda asl, bank: lambda v: v.tensor_tensor(
                    asl, asl, bank[:, 0:512], ALU.add))(asl, bank),
                      reads=(bname, f"acc:{o}"), writes=(f"acc:{o}",))

        if PHASES < 4:
            _finish(kb, nc, stack)
            return nc

        xst = [(V(ARH, 0, 2048), V(ARH, 2048, 2048, BF16, 0, 64)),
               (V(ARH, 4096, 2048), V(ARH, 6144, 2048, BF16, 0, 64))]
        XT = [V(ARH, 8192, 3072), V(ARH, 11264, 3072)]
        W13H = [V(ARQ, i * 2048, 2048) for i in range(8)]
        W2sl = [V(ARQ, 16384, 4096), V(WK, 5632, 4096)]
        scrG = V(WK, 0, 1024, F32)
        actb = [V(WK, 1024, 1024), V(WK, 2048, 1024)]
        actTb = [V(WK, 3072, 768), V(WK, 3840, 768)]
        Ystc = [V(WK, 4608, 512), V(WK, 5120, 512)]

        arh_claim = kb.barrier_ev(*[f"hT:{kk}:{o}" for kk in range(HK) for o in range(NOT_)])
        arq4_claim = kb.barrier_ev("sd_sl",
                                   *[f"sactT:{kk}:{o}" for kk in range(8) for o in range(NOT_)])
        wk4_claim = kb.barrier_ev("sact", "scr3",
                                  *[f"sact:{o}:{n}" for o in range(NOT_) for n in range(2)])
        wk4b_claim = kb.barrier_ev("sact", "scr3",
                                   *[f"sactT:{kk}:{o}" for kk in range(8) for o in range(NOT_)])
        xdeps = tuple(f"Xscat:{o}:{k}" for o in range(NOT_) for k in range(4)) + \
                tuple(f"Xdram:{e}" for e in range(16))

        pending_w2 = []

        def emit_w2_load(e, n):
            w2b = W2sl[n % 2]
            kb.dma("gpsimd", "wload",
                   (lambda e, n, w2b: lambda g: g.dma_start(
                       out=w2b[:].rearrange("p (kk c) -> p kk c", kk=8),
                       in_=d_w2[e, n]))(e, n, w2b),
                   deps=(arq4_claim if (e == 0 and n == 0) else
                         (wk4b_claim if (e == 0 and n == 1) else ())),
                   writes=(f"W2sl:{n % 2}",))

        for e in range(NEXP):
            eb = e % 2
            x0, x1 = xst[eb]
            xt = XT[eb]
            kb.dma("scalar", "xload",
                   (lambda e, x0: lambda s: s.dma_start(
                       out=x0[:], in_=d_X[e * CAP: e * CAP + 128, :]))(e, x0),
                   deps=(arh_claim if e < 2 else ()),
                   reads=xdeps, writes=(f"x0:{eb}",))
            kb.dma("scalar", "xload",
                   (lambda e, x1: lambda s: s.dma_start(
                       out=x1[:], in_=d_X[e * CAP + 128: (e + 1) * CAP, :]))(e, x1),
                   deps=(arh_claim if e < 2 else ()),
                   reads=xdeps, writes=(f"x1:{eb}",))
            for kk in range(HK):
                pst, pr = (psT0, "psT0") if kk % 2 == 0 else (psT1, "psT1")
                kb.op("tensor", (lambda x0, kk, pst: lambda t: t.transpose(
                    pst[:], x0[:, kk * 128:(kk + 1) * 128], ident_bf[:]))(x0, kk, pst),
                      reads=(f"x0:{eb}", "ident_bf"), writes=(pr,))
                dsl = xt[:, kk * CAP: kk * CAP + 128]
                kb.op("vector", (lambda dsl, pst: lambda v: v.tensor_copy(dsl, pst[:]))(dsl, pst),
                      reads=(pr,), writes=(f"XT:{eb}",))
                pst2, pr2 = (psT1, "psT1") if kk % 2 == 0 else (psT0, "psT0")
                kb.op("tensor", (lambda x1, kk, pst2: lambda t: t.transpose(
                    pst2[0:128, 0:64], x1[:, kk * 128:(kk + 1) * 128],
                    ident_bf[0:64, 0:64]))(x1, kk, pst2),
                      reads=(f"x1:{eb}", "ident_bf"), writes=(pr2,))
                dsl2 = xt[:, kk * CAP + 128: (kk + 1) * CAP]
                kb.op("vector", (lambda dsl2, pst2: lambda v: v.tensor_copy(
                    dsl2, pst2[0:128, 0:64]))(dsl2, pst2),
                      reads=(pr2,), writes=(f"XT:{eb}",))

            for pi, pair in enumerate(((0, 2), (1, 3))):
                bidx = {pair[0]: 0, pair[1]: 2}
                for q in range(4):
                    for ni, n in enumerate(pair):
                        buf = (pi * 8 + q * 2 + ni) % 8
                        kb.dma("gpsimd", "wload",
                               (lambda e, n, q, wb: lambda g: g.dma_start(
                                   out=wb[:].rearrange("p (kk c) -> p kk c", kk=4),
                                   in_=d_w13[e, n][:, q * 4:(q + 1) * 4, :]))(
                                       e, n, q, W13H[buf]),
                               deps=(arq4_claim if (e == 0 and pi == 0 and q == 0) else ()),
                               writes=(f"W13H:{buf}",))
                for q in range(4):
                    for ni, n in enumerate(pair):
                        buf = (pi * 8 + q * 2 + ni) % 8
                        for st in range(2):
                            rows = 128 if st == 0 else 64
                            bank = ps[bidx[n] + st]

                            def w13mm(q, st, rows, bank, buf, xt=xt):
                                def f(t):
                                    last = None
                                    for kk4 in range(4):
                                        kk = q * 4 + kk4
                                        if st == 0:
                                            l = xt[:, kk * CAP: kk * CAP + 128]
                                        else:
                                            l = xt[:, kk * CAP + 128: (kk + 1) * CAP]
                                        last = t.matmul(
                                            bank[0:rows, 0:512], l,
                                            W13H[buf][:, kk4 * 512:(kk4 + 1) * 512],
                                            start=(kk == 0), stop=(kk == HK - 1))
                                    return last
                                return f
                            kb.op("tensor", w13mm(q, st, rows, bank, buf),
                                  reads=(f"XT:{eb}", f"W13H:{buf}"),
                                  writes=(f"ps{bidx[n] + st}",))
                for st in range(2):
                    rows = 128 if st == 0 else 64
                    kb.op("scalar", (lambda st, rows, b=ps[bidx[pair[0]] + st]: lambda s: s.activation(
                        scrG[0:rows, :], b[0:rows, 0:512], AF.Silu))(st, rows),
                          deps=(wk4_claim if (e == 0 and pi == 0 and st == 0) else ()),
                          reads=(f"ps{bidx[pair[0]] + st}",), writes=("scrG",))
                    ab = actb[pi]
                    dsl = ab[0:rows, (0 if st == 0 else 512):(512 if st == 0 else 1024)]
                    kb.op("vector", (lambda dsl, rows, b=ps[bidx[pair[1]] + st]: lambda v: v.tensor_mul(
                        dsl, scrG[0:rows, :], b[0:rows, 0:512]))(dsl, rows),
                          reads=("scrG", f"ps{bidx[pair[1]] + st}"),
                          writes=(f"act:{pi}",))
            for kk in range(8):
                pi = kk // 4
                ab = actb[pi]
                atb = actTb[pi]
                base = (kk % 4) * 192
                lc = (kk % 4) * 128
                pst, pr = (psT0, "psT0") if kk % 2 == 0 else (psT1, "psT1")
                kb.op("tensor", (lambda ab, lc, pst: lambda t: t.transpose(
                    pst[:], ab[0:128, lc:lc + 128], ident_bf[:]))(ab, lc, pst),
                      reads=(f"act:{pi}", "ident_bf"), writes=(pr,))
                kb.op("vector", (lambda atb, base, pst: lambda v: v.tensor_copy(
                    atb[:, base:base + 128], pst[:]))(atb, base, pst),
                      reads=(pr,), writes=(f"actT:{pi}",))
                pst2, pr2 = (psT1, "psT1") if kk % 2 == 0 else (psT0, "psT0")
                kb.op("tensor", (lambda ab, lc, pst2: lambda t: t.transpose(
                    pst2[0:128, 0:64], ab[0:64, 512 + lc: 512 + lc + 128],
                    ident_bf[0:64, 0:64]))(ab, lc, pst2),
                      reads=(f"act:{pi}", "ident_bf"), writes=(pr2,))
                kb.op("vector", (lambda atb, base, pst2: lambda v: v.tensor_copy(
                    atb[:, base + 128:base + 192], pst2[0:128, 0:64]))(atb, base, pst2),
                      reads=(pr2,), writes=(f"actT:{pi}",))

            emit_w2_load(e, 0)
            emit_w2_load(e, 1)
            for n in range(4):
                w2b = W2sl[n % 2]
                if n >= 2:
                    emit_w2_load(e, n)
                for st in range(2):
                    rows = 128 if st == 0 else 64

                    def w2mm(n, st, rows, w2b):
                        def f(t):
                            last = None
                            for kk in range(8):
                                atb = actTb[kk // 4]
                                base = (kk % 4) * 192
                                l = atb[:, base:base + 128] if st == 0 else \
                                    atb[:, base + 128:base + 192]
                                last = t.matmul(ps[4][0:rows, 0:512], l,
                                                w2b[:, kk * 512:(kk + 1) * 512],
                                                start=(kk == 0), stop=(kk == 7))
                            return last
                        return f
                    kb.op("tensor", w2mm(n, st, rows, w2b),
                          reads=("actT:0", "actT:1", f"W2sl:{n % 2}"), writes=("ps4",))
                    yc = Ystc[n % 2]
                    dsl = yc[0:rows, 0:512]
                    kb.op("vector", (lambda dsl, rows: lambda v: v.tensor_copy(
                        dsl, ps[4][0:rows, 0:512]))(dsl, rows),
                          reads=("ps4",), writes=(f"Ystc:{n % 2}",))
                    if st == 0:
                        dd = d_Y[e * CAP: e * CAP + 128, n * 512:(n + 1) * 512]
                    else:
                        dd = d_Y[e * CAP + 128: (e + 1) * CAP, n * 512:(n + 1) * 512]
                    kb.dma("scalar", "ystore",
                           (lambda dd, yc, rows: lambda s: s.dma_start(
                               out=dd, in_=yc[0:rows, 0:512]))(dd, yc, rows),
                           reads=(f"Ystc:{n % 2}",), writes=(f"Ydram:{e}",))

        if PHASES < 5:
            _finish(kb, nc, stack)
            return nc

        gtl = [V(ARH, 0, 2048), V(ARH, 2048, 2048)]
        gclaim = kb.barrier_ev("XT:0", "XT:1", "x0:0", "x0:1", "x1:0", "x1:1")
        ydeps = tuple(f"Ydram:{e}" for e in range(NEXP))
        for o in range(NOT_):
            for k in range(4):
                col = o * 4 + k
                gb = (o * 4 + k) % 2
                gt = gtl[gb]
                kb.op("vector", (lambda gt: lambda v: v.memset(gt[:], 0.0))(gt),
                      deps=(gclaim if col < 2 else ()),
                      writes=(f"gt:{gb}",))
                kb.dma("gpsimd", "gath",
                       (lambda gt, col: lambda g: g.indirect_dma_start(
                           out=gt[:], out_offset=None,
                           in_=d_Y, in_offset=IOX(addrU[:, col:col + 1], 0),
                           bounds_check=bndreg, oob_is_err=False))(gt, col),
                       reads=(f"addrU:{o}:{k}", f"gt:{gb}", "bndreg") + ydeps,
                       writes=(f"gt:{gb}",))
                asl = acc[:, o * H:(o + 1) * H]
                kb.op("vector", (lambda asl, gt, col: lambda v: v.scalar_tensor_tensor(
                    asl, gt[:], wrt[:, col:col + 1], asl, ALU.mult, ALU.add))(asl, gt, col),
                      reads=(f"gt:{gb}", f"wrt:{o}", f"acc:{o}"),
                      writes=(f"acc:{o}",))

        if PHASES < 6:
            _finish(kb, nc, stack)
            return nc

        Wosl = [V(ARQ, 0, 8192), V(ARQ, 8192, 8192)]
        xch = [V(WK, 0, 1024, F32), V(WK, 1024, 1024, F32)]
        gscr = V(WK, 2048, 1024, F32)
        m1 = V(WK, 3072, 1024, F32)
        outc = [V(WK, 4096, 1024, F32), V(WK, 5120, 1024, F32)]
        wk6_claim = kb.barrier_ev("scrG", "act:0", "act:1", "actT:0", "actT:1",
                                  "Ystc:0", "Ystc:1", "W2sl:0", "W2sl:1")
        arq6_claim = kb.barrier_ev(*[f"W13H:{i}" for i in range(8)], "W2sl:0")

        for n in range(4):
            wb = Wosl[n % 2]
            wbn = f"Wosl:{n % 2}"
            kb.dma("gpsimd", "wload",
                   (lambda n, wb: lambda g: g.dma_start(
                       out=wb[:].rearrange("p (kk c) -> p kk c", kk=HK),
                       in_=d_wo[n]))(n, wb),
                   deps=(arq6_claim if n < 2 else ()),
                   writes=(wbn,))
            for o in range(NOT_):
                xb = xch[o % 2]
                kb.dma("sync", "xin2",
                       (lambda o, n, xb: lambda e: e.dma_start(
                           out=xb[:],
                           in_=d_x[(o + 1) * 128:(o + 2) * 128, n * 512:(n + 1) * 512]))(o, n, xb),
                       deps=(wk6_claim if (n == 0 and o < 2) else ()),
                       writes=(f"xch:{o % 2}",))
                bank = ps[o % 2]
                bname = f"ps{o % 2}"

                def womm(o, n, bank, wb):
                    def f(t):
                        last = None
                        for kk in range(HK):
                            last = t.matmul(bank[:, 0:512],
                                            AR1[:, kk * TOK + o * 128: kk * TOK + (o + 1) * 128],
                                            wb[:, kk * 512:(kk + 1) * 512],
                                            start=(kk == 0), stop=(kk == HK - 1))
                        return last
                    return f
                kb.op("tensor", womm(o, n, bank, wb),
                      reads=(wbn,) + tuple(f"OT:{kk}:{o}" for kk in range(HK)),
                      writes=(bname,))
                kb.op("scalar", (lambda o, n: lambda s: s.activation(
                    gscr[:], acc[:, o * H + n * 512: o * H + (n + 1) * 512], AF.Silu))(o, n),
                      deps=(wk6_claim if (n == 0 and o == 0) else ()),
                      reads=(f"acc:{o}",), writes=("gscr",))
                kb.op("vector", (lambda bank: lambda v: v.tensor_mul(
                    m1[:], bank[:, 0:512], gscr[:]))(bank),
                      reads=(bname, "gscr"), writes=("m1",))
                oc = outc[o % 2]
                kb.op("vector", (lambda oc, xb: lambda v: v.tensor_add(
                    oc[:], xb[:], m1[:]))(oc, xb),
                      reads=("m1", f"xch:{o % 2}"), writes=(f"outc:{o % 2}",))
                kb.dma("sync", "out",
                       (lambda o, n, oc: lambda e: e.dma_start(
                           out=d_out[o * 128:(o + 1) * 128, n * 512:(n + 1) * 512],
                           in_=oc[:]))(o, n, oc),
                       reads=(f"outc:{o % 2}",), writes=())

        _finish(kb, nc, stack)
    return nc


def _finish(kb, nc, stack):
    allev = kb.all_current_events()
    for name in DEBUG:
        apobj = kb.named[name]
        shape = [apobj.shape[0], apobj.shape[1]]
        dt = apobj.dtype
        dout = nc.dram_tensor("dbg_" + name, shape, dt, kind="ExternalOutput").ap()
        kb.dma("sync", "dbg",
               (lambda dout, apobj: lambda e: e.dma_start(out=dout, in_=apobj[:]))(dout, apobj),
               deps=allev)
    kb.finalize_and_emit(stack.enter_context(nc.Block()))


def _host_masks():
    masks = np.zeros((NCORES, NOT_, 128, KW), dtype=np.float32)
    NEG = -1e30
    for c in range(NCORES):
        seg = c % 4
        for qt in range(NOT_):
            gq0 = seg * TOK + qt * 128
            k0 = gq0 - HALO
            for qr2 in range(2):
                qb = (gq0 // 64) + qr2
                for kc in range(KW // 64):
                    kbk = (k0 // 64) + kc
                    ok = (abs(kbk - qb) <= 2) and (0 <= kbk * 64 < 4096)
                    if not ok:
                        masks[c, qt, qr2 * 64:(qr2 + 1) * 64, kc * 64:(kc + 1) * 64] = NEG
    return masks


def kernel(L0_attn_hidden_states, L0_attn_Wq, L0_attn_Wk, L0_attn_Wv, L0_attn_Wo,
           L0_ffn_norm_w, L0_ffn_gate_w, L0_ffn_w13, L0_ffn_w2,
           L0_ffn_shared_gate_w, L0_ffn_shared_up_w, L0_ffn_shared_down_w):
    x = np.ascontiguousarray(np.asarray(L0_attn_hidden_states, dtype=np.float32))
    B, S, _ = x.shape
    masks = _host_masks()

    C = np.ascontiguousarray

    def tile_cols(w, kk, nn):
        return C(np.asarray(w, np.float32).reshape(kk, 128, nn, 512).transpose(2, 1, 0, 3))

    wq = np.asarray(L0_attn_Wq, np.float32).reshape(HK, 128, NHEAD, DH).transpose(2, 1, 0, 3)
    wk = np.asarray(L0_attn_Wk, np.float32).reshape(HK, 128, NHEAD, DH).transpose(2, 1, 0, 3)
    wv = np.asarray(L0_attn_Wv, np.float32).reshape(HK, 128, NHEAD, DH).transpose(2, 1, 0, 3)
    w13 = np.asarray(L0_ffn_w13, np.float32).reshape(NEXP, HK, 128, 4, 512).transpose(0, 3, 2, 1, 4)
    w2 = np.asarray(L0_ffn_w2, np.float32).reshape(NEXP, 8, 128, 4, 512).transpose(0, 3, 2, 1, 4)
    weights = {
        "Wq": C(wq), "Wk": C(wk), "Wv": C(wv),
        "Wo": tile_cols(L0_attn_Wo, HK, 4),
        "norm_w": C(np.asarray(L0_ffn_norm_w, np.float32)).reshape(1, H),
        "gate_wT": C(np.asarray(L0_ffn_gate_w, np.float32).T),
        "w13": C(w13), "w2": C(w2),
        "sgw": tile_cols(L0_ffn_shared_gate_w, HK, 2),
        "suw": tile_cols(L0_ffn_shared_up_w, HK, 2),
        "sdw": tile_cols(L0_ffn_shared_down_w, 8, 4),
    }

    in_maps = []
    for c in range(NCORES):
        b, seg = c // 4, c % 4
        lo = seg * TOK - HALO
        xh = np.zeros((THAL, H), dtype=np.float32)
        s0 = max(lo, 0)
        s1 = min(lo + THAL, S)
        xh[s0 - lo: s1 - lo, :] = x[b, s0:s1, :]
        m = {"x_halo": xh, "masks": np.ascontiguousarray(masks[c].reshape(NOT_ * 128, KW))}
        m.update(weights)
        in_maps.append(m)

    nc = build_program()
    res = run_bass_kernel_spmd(nc, in_maps, core_ids=list(range(NCORES)))

    out = np.empty((B, S, H), dtype=np.float32)
    for c in range(NCORES):
        b, seg = c // 4, c % 4
        out[b, seg * TOK:(seg + 1) * TOK, :] = np.asarray(res.results[c]["out"], dtype=np.float32)
    if DEBUG:
        kernel.debug_results = res.results
    return out
